# version 14
# baseline (speedup 1.0000x reference)
import contextlib
import numpy as np
import ml_dtypes
import concourse.bass as bass
import concourse.mybir as mybir
from concourse.bass_utils import run_bass_kernel_spmd

F32 = mybir.dt.float32
BF16 = mybir.dt.bfloat16
AF = mybir.ActivationFunctionType
ALU = mybir.AluOpType
AX = mybir.AxisListType

NCORE = 8
S, D = 4096, 1024
NKT = D // 128
PI_2 = float(np.pi / 2)
NTH = 4


def build_nc(nch=16):
    nc = bass.Bass("TRN2", target_bir_lowering=False)
    x_d = nc.dram_tensor("x", [S, D], F32, kind="ExternalInput").ap()
    wq_d = nc.dram_tensor("wq", [D, 512], F32, kind="ExternalInput").ap()
    wk_d = nc.dram_tensor("wk", [D, 512], F32, kind="ExternalInput").ap()
    wv_d = nc.dram_tensor("wv", [D, 512], F32, kind="ExternalInput").ap()
    wo_d = nc.dram_tensor("wo", [512, D], F32, kind="ExternalInput").ap()
    bet_d = nc.dram_tensor("betas", [1, 512], F32, kind="ExternalInput").ap()
    th_d = nc.dram_tensor("th", [D, D], F32, kind="ExternalInput").ap()
    thn_d = nc.dram_tensor("thn", [D, D], F32, kind="ExternalInput").ap()
    ph_d = nc.dram_tensor("ph", [D, D], F32, kind="ExternalInput").ap()
    phn_d = nc.dram_tensor("phn", [D, D], F32, kind="ExternalInput").ap()
    y_d = nc.dram_tensor("y", [S, D], F32, kind="ExternalOutput").ap()
    id_d = nc.inline_tensor(np.eye(128, dtype=ml_dtypes.bfloat16), "identd").ap()

    es = contextlib.ExitStack()
    sbt = lambda name, shape, dt: es.enter_context(nc.sbuf_tensor(name, shape, dt))
    pst = lambda name, shape: es.enter_context(nc.psum_tensor(name, shape, F32))
    sem = lambda name: es.enter_context(nc.semaphore(name))

    wq_b = sbt("wq_b", [128, 4096], BF16)
    wk_b = sbt("wk_b", [128, 4096], BF16)
    wv_r = sbt("wv_r", [128, 4096], BF16)
    wo_r = sbt("wo_r", [128, 4096], BF16)
    cos_r = sbt("cos_r", [128, 8192], BF16)
    sin_r = sbt("sin_r", [128, 8192], BF16)
    stage = sbt("stage", [128, 2048], F32)
    stageB = sbt("stageB", [128, 2048], F32)
    x_sb = sbt("x_sb", [128, 4096], F32)
    out_sb = sbt("out_sb", [128, 4096], F32)
    xn_bf = sbt("xn_bf", [128, 2048], BF16)
    xnT_sb = sbt("xnT_sb", [128, 4096], BF16)
    qT_bf = sbt("qT_bf", [128, 1024], BF16)
    sqT_bf = sbt("sqT_bf", [128, 1024], BF16)
    kT_bf = sbt("kT_bf", [128, 1024], BF16)
    sk_bf = sbt("sk_bf", [128, 1024], BF16)
    v_bf = sbt("v_bf", [128, 1024], BF16)
    e1_bf = sbt("e1_bf", [128, 4096], BF16)
    pr_bf = sbt("pr_bf", [128, 4096], BF16)
    pT_sb = sbt("pT_sb", [128, 4096], BF16)
    amn_bf = sbt("amn_bf", [128, 512], BF16)
    amT_sb = sbt("amT_sb", [128, 512], BF16)
    attT_bf = sbt("attT_bf", [128, 512], BF16)
    scr = sbt("scr", [128, 4096], F32)
    memz_bf = sbt("memz_bf", [128, 520], BF16)
    g_tile = sbt("g_tile", [128, 512], BF16)
    gcol = sbt("gcol", [128, 4], F32)
    omg = sbt("omg", [128, 4], F32)
    brow = sbt("brow", [1, 512], F32)
    grow_bf = sbt("grow_bf", [1, 512], BF16)
    onesr_bf = sbt("onesr_bf", [1, 128], BF16)
    onesk_bf = sbt("onesk_bf", [128, 1], BF16)
    ident_sb = sbt("ident_sb", [128, 128], BF16)
    stat6 = sbt("stat6", [128, 24], F32)
    smv = sbt("smv", [128, 4], F32)
    lnv = sbt("lnv", [128, 2], F32)
    rstd = sbt("rstd", [128, 2], F32)
    negr = sbt("negr", [128, 2], F32)
    negmr = sbt("negmr", [128, 2], F32)
    negm = sbt("negm", [128, 16], F32)
    negms = sbt("negms", [128, 16], F32)
    den = sbt("den", [128, 16], F32)
    rdb = sbt("rdb", [128, 16], F32)
    rmb = sbt("rmb", [128, 8], F32)
    m2t = sbt("m2t", [128, 512], F32)
    cbias = sbt("cbias", [128, 3], F32)
    fscr = sbt("fscr", [128, 2048], BF16)
    magb = sbt("magb", [128, 2048], BF16)

    pb = [pst(f"pb{i}", [128, 512]) for i in range(8)]

    s_x = [sem("s_x0"), sem("s_x1")]; s_o = [sem("s_o0"), sem("s_o1")]; s_mcp = sem("s_mcp")
    s_wg = sem("s_wg"); s_bld = sem("s_bld")
    s_tb = [sem("s_tb0"), sem("s_tb1")]; s_ta = [sem("s_ta0"), sem("s_ta1")]
    s_ia = sem("s_ia"); s_iv = sem("s_iv"); s_ip = sem("s_ip"); s_ig = sem("s_ig")
    s_ln = sem("s_ln"); s_xa = sem("s_xa"); s_lnc = sem("s_lnc")
    s_xnt = sem("s_xnt"); s_xne = sem("s_xne")
    s_qt = sem("s_qt"); s_qte = sem("s_qte")
    s_mq = sem("s_mq"); s_eq = sem("s_eq"); s_sq = sem("s_sq")
    s_kt = sem("s_kt"); s_kte = sem("s_kte"); s_knt = sem("s_knt")
    s_mk = sem("s_mk"); s_ek = sem("s_ek"); s_sk = sem("s_sk")
    s_v = sem("s_v"); s_ve = sem("s_ve")
    s_sc = sem("s_sc"); s_mx = sem("s_mx"); s_e1 = sem("s_e1")
    s_pr = sem("s_pr"); s_pt = sem("s_pt"); s_pte = sem("s_pte")
    s_me = sem("s_me"); s_am = sem("s_am"); s_amn = sem("s_amn")
    s_amt = sem("s_amt"); s_ad = sem("s_ad"); s_bl = sem("s_bl")
    s_wo = sem("s_wo"); s_r1 = sem("s_r1")
    s_ff = sem("s_ff"); s_sqf = sem("s_sqf"); s_ss = sem("s_ss")
    s_mag = sem("s_mag"); s_r2 = sem("s_r2"); s_md = sem("s_md")

    blk = es.enter_context(nc.Block())
    with es:
        def ffn_mms(t_, tf, u):
            ppf = 2048 * (tf % 2)
            for mi, w_s in enumerate((cos_r, sin_r)):
                if tf == 0 and u == 0:
                    t_.wait_ge(s_ia, 6 if mi == 0 else 10)
                for half in range(2):
                    n = 8 * tf + 4 * u + 2 * mi + half
                    if n >= 2:
                        t_.wait_ge(s_sqf, n - 1)
                    for dt in range(NKT):
                        li = t_.matmul(pb[6 + half][:, :],
                                  xnT_sb[:, ppf + 256 * dt + 128 * u:
                                         ppf + 256 * dt + 128 * u + 128],
                                  w_s[:, 1024 * dt + 512 * half:
                                      1024 * dt + 512 * half + 512],
                                  start=(dt == 0), stop=(dt == NKT - 1))
                    li.then_inc(s_ff, 1)

        def ffn_evac(sc, tf, u):
            if tf >= 1 or u >= 1:
                sc.wait_ge(s_ss, 2 * tf + u)
            sc.drain()
            for mi in range(2):
                for half in range(2):
                    sc.wait_ge(s_ff, 8 * tf + 4 * u + 2 * mi + half + 1)
                    sc.activation(fscr[:, 1024 * mi + 512 * half:
                                       1024 * mi + 512 * half + 512],
                                  pb[6 + half][:, :],
                                  AF.Copy).then_inc(s_sqf, 1)

        def ffn_mag(sc, tf, u):
            if tf >= 1:
                sc.wait_ge(s_r2, 2 * (tf - 1) + u + 1)
            sc.wait_ge(s_ss, 2 * tf + u + 1)
            sc.drain()
            sc.activation(stage[:, 1024:2048], stage[:, 0:1024], AF.Ln,
                          bias=cbias[:, 2:3])
            sc.drain()
            sc.activation(magb[:, 1024 * u:1024 * u + 1024],
                          stage[:, 1024:2048], AF.Exp,
                          scale=0.5).then_inc(s_mag, 1)

        def ffn_dve(v, tf, u):
            ppf = 2048 * (tf % 2)
            v.wait_ge(s_mag, 2 * tf + u + 1)
            v.tensor_add(out_sb[:, ppf + 1024 * u:ppf + 1024 * u + 1024],
                         out_sb[:, ppf + 1024 * u:ppf + 1024 * u + 1024],
                         magb[:, 1024 * u:1024 * u + 1024]).then_inc(s_r2, 1)

        @blk.sync
        def _(s):
            s.dma_start(ident_sb[:, :], id_d[:, :]).then_inc(s_bld, 16)
            s.dma_start(brow[:, :], bet_d[:, :]).then_inc(s_bld, 16)
            for t in range(min(2, nch)):
                src = x_d[256 * t:256 * (t + 1), :].rearrange(
                    "(u p) d -> p u d", u=2)
                dstx = x_sb[:, 2048 * (t % 2):2048 * (t % 2) + 2048].rearrange(
                    "p (u d) -> p u d", u=2)
                s.dma_start(dstx, src).then_inc(s_x[t % 2], 16)
            for t in range(nch):
                if t >= 1:
                    s.wait_ge(s_me, t)
                    s.dma_start(memz_bf[64:128, :],
                                memz_bf[0:64, :]).then_inc(s_mcp, 16)
                if t + 2 < nch:
                    s.wait_ge(s_r1, 2 * t + 2)
                    tt = t + 2
                    src = x_d[256 * tt:256 * (tt + 1), :].rearrange(
                        "(u p) d -> p u d", u=2)
                    dstx = x_sb[:, 2048 * (tt % 2):
                                2048 * (tt % 2) + 2048].rearrange(
                        "p (u d) -> p u d", u=2)
                    s.dma_start(dstx, src).then_inc(s_x[t % 2], 16)
                if t >= 1:
                    s.wait_ge(s_r2, 2 * t)
                    tt = t - 1
                    dst = y_d[256 * tt:256 * (tt + 1), :].rearrange(
                        "(u p) d -> p u d", u=2)
                    srco = out_sb[:, 2048 * (tt % 2):
                                  2048 * (tt % 2) + 2048].rearrange(
                        "p (u d) -> p u d", u=2)
                    s.dma_start(dst, srco).then_inc(s_o[tt % 2], 16)
            s.wait_ge(s_r2, 2 * nch)
            tt = nch - 1
            dst = y_d[256 * tt:256 * (tt + 1), :].rearrange(
                "(u p) d -> p u d", u=2)
            srco = out_sb[:, 2048 * (tt % 2):2048 * (tt % 2) + 2048].rearrange(
                "p (u d) -> p u d", u=2)
            s.dma_start(dst, srco).then_inc(s_o[tt % 2], 16)
            s.wait_ge(s_o[0], 16 * ((nch + 1) // 2))
            s.wait_ge(s_o[1], 16 * (nch // 2))

        @blk.gpsimd
        def _(g):
            g.memset(memz_bf[:, :], 0.0)
            g.drain()
            g.memset(memz_bf[:, :].rearrange("p (h c) -> p h c", c=65)
                     [:, :, 64:65], 1.0 / 64.0)
            g.memset(onesr_bf[:, :], 1.0)
            g.memset(cbias[:, 0:1], PI_2)
            g.memset(cbias[:, 1:2], 1e-5)
            g.memset(cbias[:, 2:3], 1e-38)
            g.memset(onesk_bf[:, :], 1.0)
            g.drain().then_inc(s_ig, 1)
            for w_d, w_s in ((wq_d, wq_b), (wk_d, wk_b), (wv_d, wv_r),
                             (wo_d, wo_r)):
                g.dma_start(w_s[:, :].rearrange("p (k m) -> p k m",
                                                 k=w_d.shape[0] // 128),
                            w_d[:, :].rearrange("(k p) m -> p k m",
                                                p=128)).then_inc(s_wg, 16)
            srcs = []
            for a_d, n_d in ((th_d, thn_d), (ph_d, phn_d)):
                for p in range(NTH):
                    srcs.append((
                        a_d[256 * p:256 * (p + 1), :].rearrange(
                            "(gg pp) d -> pp gg d", pp=128),
                        n_d[256 * p:256 * (p + 1), :].rearrange(
                            "(gg pp) d -> pp gg d", pp=128)))
            stg3 = [stage[:, :].rearrange("p (gg d) -> p gg d", gg=2),
                    stageB[:, :].rearrange("p (gg d) -> p gg d", gg=2)]

            def emit_accum(m):
                g.wait_ge(s_tb[m % 2], 16 * (m // 2 + 1))
                g.dma_start(stg3[m % 2], srcs[m][1],
                            accum_op=ALU.add).then_inc(s_ta[m % 2], 16)

            for n in range(2 * NTH):
                if n >= 2:
                    g.wait_ge(s_ia, n + 1)
                g.dma_start(stg3[n % 2], srcs[n][0]).then_inc(s_tb[n % 2], 16)
                if n >= 1:
                    emit_accum(n - 1)
            emit_accum(2 * NTH - 1)
            for t in range(nch):
                for u in range(2):
                    if t == 0 and u == 0:
                        g.wait_ge(s_ia, 10)
                    if 2 * t + u >= 1:
                        g.wait_ge(s_mag, 2 * t + u)
                    g.wait_ge(s_sqf, 8 * t + 4 * u + 2)
                    g.tensor_mul(stage[:, 0:1024], fscr[:, 0:1024],
                                 fscr[:, 0:1024])
                    g.wait_ge(s_sqf, 8 * t + 4 * u + 4)
                    g.drain()
                    g.tensor_mul(stage[:, 1024:2048], fscr[:, 1024:2048],
                                 fscr[:, 1024:2048])
                    g.drain()
                    g.tensor_add(stage[:, 0:1024], stage[:, 0:1024],
                                 stage[:, 1024:2048]).then_inc(s_ss, 1)

        @blk.scalar
        def _(sc):
            sc.wait_ge(s_bld, 32)
            sc.activation(grow_bf[:, :], brow[:, :], AF.Sigmoid).then_inc(s_ia, 1)
            sc.drain().then_inc(s_ia, 1)

            def emit_sin(n):
                i, p = divmod(n, NTH)
                dst = cos_r if i == 0 else sin_r
                buf = stage if n % 2 == 0 else stageB
                sc.wait_ge(s_ta[n % 2], 16 * (n // 2 + 1))
                if n == 0:
                    sc.wait_ge(s_ig, 1)
                if i == 0:
                    sc.activation(dst[:, 2048 * p:2048 * (p + 1)],
                                  buf[:, :], AF.Sin,
                                  bias=cbias[:, 0:1]).then_inc(s_ia, 1)
                else:
                    sc.activation(dst[:, 2048 * p:2048 * (p + 1)],
                                  buf[:, :], AF.Sin).then_inc(s_ia, 1)

            smv3 = smv[:, :].rearrange("p (u c) -> p u c", c=2)

            def ln_chain(tt):
                sc.wait_ge(s_ln, tt + 1)
                if tt >= 1:
                    sc.wait_ge(s_xa, 2 * tt)
                sc.activation(lnv[:, :], smv3[:, :, 1:2], AF.Ln, bias=cbias[:, 1:2])
                sc.drain()
                sc.activation(rstd[:, :], lnv[:, :], AF.Exp, scale=-0.5)
                sc.drain()
                sc.activation(negr[:, :], rstd[:, :], AF.Copy, scale=-1.0)
                sc.drain()
                for u in range(2):
                    sc.activation(negmr[:, u:u + 1], smv[:, 2 * u:2 * u + 1],
                                  AF.Copy, scale=negr[:, u:u + 1])
                sc.drain().then_inc(s_lnc, 1)

            ln_chain(0)
            for n in range(2 * NTH):
                emit_sin(n)
            for t in range(nch):
                pp = 2048 * (t % 2)
                sc.wait_ge(s_mq, t + 1)
                sc.activation(scr[:, 1024:2048], scr[:, 0:1024],
                              AF.Exp).then_inc(s_eq, 1)
                sc.wait_ge(s_mk, t + 1)
                sc.activation(scr[:, 3072:4096], scr[:, 2048:3072],
                              AF.Exp).then_inc(s_ek, 1)
                for u in range(2):
                    sc.wait_ge(s_mx, 2 * t + u + 1)
                    for h in range(8):
                        jb = 2 * (h % 2) + (h // 4)
                        ib = (h // 2) % 2
                        c = 8 * u + 2 * jb + ib
                        sl = pb[2 + jb][:, 256 * ib:256 * ib + 256]
                        sc.activation(e1_bf[:, 2048 * u + 256 * h:
                                            2048 * u + 256 * h + 256],
                                      sl, AF.Exp, bias=negms[:, c:c + 1],
                                      scale=0.125, accum_out=den[:, c:c + 1])
                    sc.drain().then_inc(s_e1, 1)
                    if t >= 1:
                        ffn_evac(sc, t - 1, u)
                        if u == 1:
                            ffn_mag(sc, t - 1, 0)
                            ffn_mag(sc, t - 1, 1)
                    if u == 0 and t + 1 < nch:
                        ln_chain(t + 1)
            for u in range(2):
                ffn_evac(sc, nch - 1, u)
            ffn_mag(sc, nch - 1, 0)
            ffn_mag(sc, nch - 1, 1)

        @blk.tensor
        def _(t_):
            t_.wait_ge(s_ia, 2)
            t_.wait_ge(s_ig, 1)
            t_.wait_ge(s_bld, 32)
            t_.matmul(pb[2][:, :], onesr_bf[0:1, :], grow_bf[0:1, :],
                      start=True, stop=True).then_inc(s_ip, 1)
            for ft in range(4):
                t_.matmul(pb[3][:, ft:ft + 1],
                          grow_bf[0:1, 128 * ft:128 * (ft + 1)],
                          onesr_bf[0:1, 0:1], start=True,
                          stop=True).then_inc(s_ip, 1)
            t_.wait_ge(s_iv, 2)
            for t in range(nch):
                pp = 2048 * (t % 2)
                t_.wait_ge(s_xa, 2 * t + 2)
                if t >= 1:
                    t_.wait_ge(s_amn, 2 * t)
                    t_.wait_ge(s_bl, 2 * t)
                for dt in range(NKT):
                    if dt == 4 and t >= 1:
                        t_.wait_ge(s_r1, 2 * t)
                    for u in range(2):
                        li = t_.matmul(pb[2 + dt // 2][:, 128 * (2 * (dt % 2) + u):
                                                  128 * (2 * (dt % 2) + u) + 128],
                                  xn_bf[:, 1024 * u + 128 * dt:
                                        1024 * u + 128 * dt + 128],
                                  ident_sb[:, :], start=True, stop=True)
                li.then_inc(s_xnt, 1)
                t_.wait_ge(s_xne, t + 1)
                if t == 0:
                    t_.wait_ge(s_wg, 64)
                for mt in range(4):
                    for kt in range(NKT):
                        li = t_.matmul(pb[2 + mt // 2][:, 256 * (mt % 2):
                                                  256 * (mt % 2) + 256],
                                  wq_b[:, 512 * kt + 128 * mt:
                                       512 * kt + 128 * mt + 128],
                                  xnT_sb[:, pp + 256 * kt:pp + 256 * kt + 256],
                                  start=(kt == 0), stop=(kt == NKT - 1))
                li.then_inc(s_qt, 1)
                if t == 0:
                    t_.wait_ge(s_wg, 64)
                for mt in range(4):
                    for kt in range(NKT):
                        li = t_.matmul(pb[4 + mt // 2][:, 256 * (mt % 2):
                                                  256 * (mt % 2) + 256],
                                  wk_b[:, 512 * kt + 128 * mt:
                                       512 * kt + 128 * mt + 128],
                                  xnT_sb[:, pp + 256 * kt:pp + 256 * kt + 256],
                                  start=(kt == 0), stop=(kt == NKT - 1))
                li.then_inc(s_kt, 1)
                t_.wait_ge(s_qte, t + 1)
                if t == 0:
                    t_.wait_ge(s_wg, 64)
                for u in range(2):
                    for kt in range(NKT):
                        li = t_.matmul(pb[2 + u][:, :],
                                  xnT_sb[:, pp + 256 * kt + 128 * u:
                                         pp + 256 * kt + 128 * u + 128],
                                  wv_r[:, 512 * kt:512 * kt + 512],
                                  start=(kt == 0), stop=(kt == NKT - 1))
                    li.then_inc(s_v, 1)
                t_.wait_ge(s_kte, t + 1)
                for u in range(2):
                    for ft in range(4):
                        li = t_.matmul(pb[4 + u][:, 128 * ft:128 * ft + 128],
                                  kT_bf[:, 256 * ft + 128 * u:
                                        256 * ft + 128 * u + 128],
                                  ident_sb[:, :], start=True, stop=True)
                li.then_inc(s_knt, 1)
                for u in range(2):
                    t_.wait_ge(s_ve, 2 * t + 2)
                    t_.wait_ge(s_sk, t + 1)
                    if u == 1:
                        t_.wait_ge(s_pte, 2 * t + 1)
                    for h in (0, 2, 4, 6, 1, 3, 5, 7):
                        b = 64 * (h % 2)
                        jb = 2 * (h % 2) + (h // 4)
                        ib = (h // 2) % 2
                        li = t_.matmul(pb[2 + jb][:, 256 * ib:256 * ib + 256],
                                  qT_bf[b:b + 64, 256 * (h // 2) + 128 * u:
                                        256 * (h // 2) + 128 * u + 128],
                                  kT_bf[b:b + 64, 256 * (h // 2):
                                        256 * (h // 2) + 256],
                                  start=True, stop=True)
                    li.then_inc(s_sc, 1)
                    if t >= 1:
                        ffn_mms(t_, t - 1, u)
                    t_.wait_ge(s_pr, 2 * t + u + 1)
                    for h in range(8):
                        for kt in range(2):
                            li = t_.matmul(pb[2 + h // 2][:, 256 * (h % 2) + 128 * kt:
                                                     256 * (h % 2) + 128 * kt + 128],
                                      pr_bf[:, 2048 * u + 256 * h + 128 * kt:
                                            2048 * u + 256 * h + 128 * kt + 128],
                                      ident_sb[:, :], start=True, stop=True)
                    li.then_inc(s_pt, 1)
                for u in range(2):
                    bank_ad = pb[2] if u == 0 else pb[4]
                    t_.wait_ge(s_pte, 2 * t + 2)
                    if u == 1:
                        t_.wait_ge(s_bl, 2 * t + 1)
                        t_.wait_ge(s_amn, 2 * t + 1)
                        t_.wait_ge(s_r1, 2 * t + 1)
                    for kt in range(2):
                        for h in range(8):
                            b = 64 * (h % 2)
                            li = t_.matmul(bank_ad[b:b + 64, 128 * (h // 2):
                                              128 * (h // 2) + 128],
                                      v_bf[:, 512 * kt + 64 * h:
                                           512 * kt + 64 * h + 64],
                                      pT_sb[:, 2048 * u + 1024 * kt + 128 * h:
                                            2048 * u + 1024 * kt + 128 * h + 128],
                                      start=(kt == 0 and h <= 1),
                                      stop=(kt == 1 and h >= 6),
                                      skip_group_check=True)
                    li.then_inc(s_ad, 1)
                    if t >= 1:
                        t_.wait_ge(s_me, t)
                        t_.wait_ge(s_mcp, 16 * t)
                    else:
                        t_.wait_ge(s_ig, 1)
                    t_.wait_ge(s_sq, t + 1)
                    bkA = pb[3] if u == 0 else pb[5]
                    bkB = pb[4] if u == 0 else pb[2]
                    for h in range(8):
                        bank = bkA if h % 2 == 0 else bkB
                        col = 65 * (h // 2)
                        b = 64 * (h % 2)
                        li = t_.matmul(bank[:, col:col + 65],
                                  sqT_bf[b:b + 64, 256 * (h // 2) + 128 * u:
                                         256 * (h // 2) + 128 * u + 128],
                                  memz_bf[b:b + 64, 65 * h:65 * h + 65],
                                  start=True, stop=True)
                    li.then_inc(s_am, 1)
                    bank_mt = pb[5] if u == 0 else pb[3]
                    t_.wait_ge(s_amn, 2 * t + u + 1)
                    for ft in range(4):
                        li = t_.matmul(bank_mt[:, 128 * ft:128 * ft + 128],
                                  amn_bf[:, 128 * ft:128 * ft + 128],
                                  ident_sb[:, :], start=True, stop=True)
                    li.then_inc(s_amt, 1)
                    t_.wait_ge(s_bl, 2 * t + u + 1)
                    if t == 0 and u == 0:
                        t_.wait_ge(s_wg, 64)
                    bh0 = pb[2] if u == 0 else pb[4]
                    bh1 = pb[3] if u == 0 else pb[5]
                    for half in range(2):
                        for ft in range(4):
                            li = t_.matmul((bh0 if half == 0 else bh1)[:, :],
                                      attT_bf[:, 128 * ft:128 * ft + 128],
                                      wo_r[:, 1024 * ft + 512 * half:
                                           1024 * ft + 512 * half + 512],
                                      start=(ft == 0), stop=(ft == 3))
                    li.then_inc(s_wo, 1)
                if t >= 1:
                    t_.wait_ge(s_me, t)
                for kt in range(2):
                    for h in range(8):
                        t_.matmul(pb[0][0:64, 64 * h:64 * h + 64],
                                  sk_bf[:, 512 * kt + 64 * h:
                                        512 * kt + 64 * h + 64],
                                  v_bf[:, 512 * kt + 64 * h:
                                       512 * kt + 64 * h + 64],
                                  start=(t == 0 and kt == 0 and h == 0),
                                  stop=(t == nch - 1 and kt == 1 and h == 7),
                                  skip_group_check=True)
                        li = t_.matmul(pb[1][0:64, h:h + 1],
                                  sk_bf[:, 512 * kt + 64 * h:
                                        512 * kt + 64 * h + 64],
                                  onesk_bf[:, :],
                                  start=(t == 0 and kt == 0 and h == 0),
                                  stop=(t == nch - 1 and kt == 1 and h == 7),
                                  skip_group_check=True)
                li.then_inc(s_md, 1)
            for u in range(2):
                ffn_mms(t_, nch - 1, u)

        @blk.vector
        def _(v):
            v.wait_ge(s_ip, 1)
            v.tensor_copy(g_tile[:, :], pb[2][:, :]).then_inc(s_iv, 1)
            v.wait_ge(s_ip, 5)
            v.tensor_copy(gcol[:, :], pb[3][:, 0:4])
            v.drain()
            v.tensor_scalar(omg[:, :], gcol[:, :], -1.0, 1.0, ALU.mult, ALU.add)
            v.drain().then_inc(s_iv, 1)
            def bn_xn_stats(tt):
                pq = 2048 * (tt % 2)
                v.wait_ge(s_x[tt % 2], 16 * (tt // 2 + 1))
                if tt >= 1:
                    v.wait_ge(s_lnc, tt)
                for u in range(2):
                    v.bn_stats(stat6[:, 12 * u:12 * u + 6],
                               x_sb[:, pq + 1024 * u:pq + 1024 * u + 512])
                    v.bn_stats(stat6[:, 12 * u + 6:12 * u + 12],
                               x_sb[:, pq + 1024 * u + 512:
                                    pq + 1024 * u + 1024])
                    v.drain()
                    li = v.bn_aggr(smv[:, 2 * u:2 * u + 2],
                                   stat6[:, 12 * u:12 * u + 12])
                li.then_inc(s_ln, 1)

            def bn_xn_affine(tt):
                pq = 2048 * (tt % 2)
                v.wait_ge(s_lnc, tt + 1)
                if tt >= 1:
                    v.wait_ge(s_xnt, tt)
                for u in range(2):
                    v.tensor_scalar(xn_bf[:, 1024 * u:1024 * (u + 1)],
                                    x_sb[:, pq + 1024 * u:pq + 1024 * (u + 1)],
                                    rstd[:, u:u + 1], negmr[:, u:u + 1],
                                    ALU.mult, ALU.add).then_inc(s_xa, 1)

            bn_xn_stats(0)
            bn_xn_affine(0)
            for t in range(nch):
                pp = 2048 * (t % 2)
                v.wait_ge(s_xnt, t + 1)
                if t >= 2:
                    v.wait_ge(s_ff, 8 * (t - 2) + 8)
                for j in range(4):
                    li = v.tensor_copy(xnT_sb[:, pp + 512 * j:pp + 512 * j + 512],
                                       pb[2 + j][:, :])
                li.then_inc(s_xne, 1)
                v.wait_ge(s_qt, t + 1)
                v.tensor_copy(qT_bf[:, 0:512], pb[2][:, :])
                v.tensor_copy(qT_bf[:, 512:1024],
                              pb[3][:, :]).then_inc(s_qte, 1)
                v.drain()
                v.tensor_scalar_min(scr[:, 0:1024], qT_bf[:, :],
                                    0.0).then_inc(s_mq, 1)
                v.wait_ge(s_kt, t + 1)
                v.tensor_copy(kT_bf[:, 0:512], pb[4][:, :])
                v.tensor_copy(kT_bf[:, 512:1024],
                              pb[5][:, :]).then_inc(s_kte, 1)
                v.wait_ge(s_eq, t + 1)
                v.scalar_tensor_tensor(sqT_bf[:, :], qT_bf[:, :], 0.0,
                                       scr[:, 1024:2048], ALU.max,
                                       ALU.add).then_inc(s_sq, 1)
                for u in range(2):
                    v.wait_ge(s_v, 2 * t + u + 1)
                    v.tensor_copy(v_bf[:, 512 * u:512 * u + 512],
                                  pb[2 + u][:, :]).then_inc(s_ve, 1)
                v.wait_ge(s_knt, t + 1)
                for u in range(2):
                    li = v.tensor_scalar_min(scr[:, 2048 + 512 * u:
                                                 2048 + 512 * u + 512],
                                             pb[4 + u][:, :], 0.0)
                li.then_inc(s_mk, 1)
                v.wait_ge(s_ek, t + 1)
                for u in range(2):
                    li = v.scalar_tensor_tensor(sk_bf[:, 512 * u:512 * u + 512],
                                                pb[4 + u][:, :], 0.0,
                                                scr[:, 3072 + 512 * u:
                                                    3072 + 512 * u + 512],
                                                ALU.max, ALU.add)
                li.then_inc(s_sk, 1)
                if t >= 1:
                    v.wait_ge(s_md, t)
                    mz3 = memz_bf[0:64, :].rearrange("p (h c) -> p h c", c=65)
                    v.tensor_copy(mz3[:, :, 0:64],
                                  pb[0][0:64, :].rearrange("p (h c) -> p h c",
                                                           c=64))
                    v.tensor_scalar_add(mz3[:, :, 64:65],
                                        pb[1][0:64, 0:8].unsqueeze(2),
                                        1.0 / 64.0).then_inc(s_me, 1)
                for u in range(2):
                    v.wait_ge(s_sc, 2 * t + u + 1)
                    for j in range(4):
                        v.reduce_max(negm[:, 8 * u + 2 * j:8 * u + 2 * j + 2],
                                     pb[2 + j][:, :].rearrange(
                                         "p (i c) -> p i c", c=256),
                                     axis=AX.X, negate=True)
                    v.drain()
                    v.tensor_scalar_mul(negms[:, 8 * u:8 * u + 8],
                                        negm[:, 8 * u:8 * u + 8],
                                        0.125).then_inc(s_mx, 1)
                    if u == 0 and t + 1 < nch:
                        bn_xn_stats(t + 1)
                    v.wait_ge(s_e1, 2 * t + u + 1)
                    v.reciprocal(rdb[:, 8 * u:8 * u + 8],
                                 den[:, 8 * u:8 * u + 8])
                    v.drain()
                    for h in range(8):
                        jb = 2 * (h % 2) + (h // 4)
                        ib = (h // 2) % 2
                        c = 8 * u + 2 * jb + ib
                        li = v.tensor_scalar_mul(
                            pr_bf[:, 2048 * u + 256 * h:2048 * u + 256 * h + 256],
                            e1_bf[:, 2048 * u + 256 * h:2048 * u + 256 * h + 256],
                            rdb[:, c:c + 1])
                    li.then_inc(s_pr, 1)
                    v.wait_ge(s_pt, 2 * t + u + 1)
                    for j in range(4):
                        src = pb[2 + j][:, :].rearrange(
                            "p (i kt q) -> p i kt q", i=2, kt=2)
                        dst = pT_sb[:, 2048 * u:2048 * u + 2048].rearrange(
                            "p (kt h q) -> p h kt q", kt=2, h=8)[:, 2 * j:2 * j + 2]
                        li = v.tensor_copy(dst, src)
                    li.then_inc(s_pte, 1)
                    if u == 1 and t + 1 < nch:
                        bn_xn_affine(t + 1)
                for u in range(2):
                    bkA = pb[3] if u == 0 else pb[5]
                    bkB = pb[4] if u == 0 else pb[2]
                    bank_mt = pb[5] if u == 0 else pb[3]
                    bank_ad = pb[2] if u == 0 else pb[4]
                    v.wait_ge(s_am, 2 * t + u + 1)
                    a3 = bkA[:, 0:260].rearrange("p (j c) -> p j c", c=65)
                    b3 = bkB[:, 0:260].rearrange("p (j c) -> p j c", c=65)
                    rm3 = rmb[:, 0:8].rearrange("p (j i) -> p j i", i=2)
                    v.reciprocal(rm3[:, :, 0:1], a3[:, :, 64:65])
                    v.reciprocal(rm3[:, :, 1:2], b3[:, :, 64:65])
                    v.drain()
                    v.tensor_tensor(m2t[:, :],
                                    rmb[:, 0:8].unsqueeze(2).broadcast_to(
                                        [128, 8, 64]),
                                    g_tile[:, :].rearrange("p (h c) -> p h c",
                                                           c=64), ALU.mult)
                    v.drain()
                    amn4 = amn_bf[:, :].rearrange("p (j i c) -> p j i c",
                                                  i=2, c=64)
                    m24 = m2t[:, :].rearrange("p (j i c) -> p j i c",
                                              i=2, c=64)
                    v.tensor_tensor(amn4[:, :, 0:1, :],
                                    a3[:, :, 0:64].unsqueeze(2),
                                    m24[:, :, 0:1, :], ALU.mult)
                    v.tensor_tensor(amn4[:, :, 1:2, :],
                                    b3[:, :, 0:64].unsqueeze(2),
                                    m24[:, :, 1:2, :],
                                    ALU.mult).then_inc(s_amn, 1)
                    v.wait_ge(s_amt, 2 * t + u + 1)
                    v.tensor_copy(amT_sb[:, :], bank_mt[:, :])
                    v.drain()
                    v.wait_ge(s_ad, 2 * t + u + 1)
                    for ft in range(4):
                        li = v.scalar_tensor_tensor(
                            attT_bf[:, 128 * ft:128 * ft + 128],
                            bank_ad[:, 128 * ft:128 * ft + 128],
                            omg[:, ft:ft + 1],
                            amT_sb[:, 128 * ft:128 * ft + 128],
                            ALU.mult, ALU.add)
                    li.then_inc(s_bl, 1)
                    v.wait_ge(s_wo, 2 * t + u + 1)
                    if t >= 2 and u == 0:
                        v.wait_ge(s_o[t % 2], 16 * (t // 2))
                    bh0 = pb[2] if u == 0 else pb[4]
                    bh1 = pb[3] if u == 0 else pb[5]
                    v.tensor_add(out_sb[:, pp + 1024 * u:pp + 1024 * u + 512],
                                 x_sb[:, pp + 1024 * u:pp + 1024 * u + 512],
                                 bh0[:, :])
                    v.tensor_add(out_sb[:, pp + 1024 * u + 512:
                                        pp + 1024 * u + 1024],
                                 x_sb[:, pp + 1024 * u + 512:
                                      pp + 1024 * u + 1024],
                                 bh1[:, :]).then_inc(s_r1, 1)
                if u == 1 and t >= 1:
                    ffn_dve(v, t - 1, 0)
                    ffn_dve(v, t - 1, 1)
            for u in range(2):
                ffn_dve(v, nch - 1, u)

    return nc


def kernel(L0_attn_x, L0_attn_Wq, L0_attn_Wk, L0_attn_Wv, L0_attn_betas,
           L0_attn_Wo, L0_ffn_theta, L0_ffn_phi, L0_ffn_theta_noise,
           L0_ffn_phi_noise):
    x = np.asarray(L0_attn_x)
    nc = build_nc()
    common = {
        "wq": np.ascontiguousarray(L0_attn_Wq),
        "wk": np.ascontiguousarray(L0_attn_Wk),
        "wv": np.ascontiguousarray(L0_attn_Wv),
        "wo": np.ascontiguousarray(L0_attn_Wo),
        "betas": np.ascontiguousarray(np.asarray(L0_attn_betas).reshape(1, 512)),
        "th": np.ascontiguousarray(L0_ffn_theta),
        "thn": np.ascontiguousarray(L0_ffn_theta_noise),
        "ph": np.ascontiguousarray(L0_ffn_phi),
        "phn": np.ascontiguousarray(L0_ffn_phi_noise),
    }
    in_maps = [dict(common, x=np.ascontiguousarray(x[b])) for b in range(NCORE)]
    res = run_bass_kernel_spmd(nc, in_maps, core_ids=list(range(NCORE)))
    out = np.stack([np.asarray(res.results[b]["y"]) for b in range(NCORE)],
                   axis=0)
    return out


# version 15
# speedup vs baseline: 1.0020x; 1.0020x over previous
import contextlib
import numpy as np
import ml_dtypes
import concourse.bass as bass
import concourse.mybir as mybir
from concourse.bass_utils import run_bass_kernel_spmd

F32 = mybir.dt.float32
BF16 = mybir.dt.bfloat16
AF = mybir.ActivationFunctionType
ALU = mybir.AluOpType
AX = mybir.AxisListType

NCORE = 8
S, D = 4096, 1024
NKT = D // 128
PI_2 = float(np.pi / 2)
NTH = 4


def build_nc(nch=16):
    nc = bass.Bass("TRN2", target_bir_lowering=False)
    x_d = nc.dram_tensor("x", [S, D], F32, kind="ExternalInput").ap()
    wq_d = nc.dram_tensor("wq", [D, 512], F32, kind="ExternalInput").ap()
    wk_d = nc.dram_tensor("wk", [D, 512], F32, kind="ExternalInput").ap()
    wv_d = nc.dram_tensor("wv", [D, 512], F32, kind="ExternalInput").ap()
    wo_d = nc.dram_tensor("wo", [512, D], F32, kind="ExternalInput").ap()
    bet_d = nc.dram_tensor("betas", [1, 512], F32, kind="ExternalInput").ap()
    th_d = nc.dram_tensor("th", [D, D], F32, kind="ExternalInput").ap()
    thn_d = nc.dram_tensor("thn", [D, D], F32, kind="ExternalInput").ap()
    ph_d = nc.dram_tensor("ph", [D, D], F32, kind="ExternalInput").ap()
    phn_d = nc.dram_tensor("phn", [D, D], F32, kind="ExternalInput").ap()
    y_d = nc.dram_tensor("y", [S, D], F32, kind="ExternalOutput").ap()
    id_d = nc.inline_tensor(np.eye(128, dtype=ml_dtypes.bfloat16), "identd").ap()

    es = contextlib.ExitStack()
    sbt = lambda name, shape, dt: es.enter_context(nc.sbuf_tensor(name, shape, dt))
    pst = lambda name, shape: es.enter_context(nc.psum_tensor(name, shape, F32))
    sem = lambda name: es.enter_context(nc.semaphore(name))

    wq_b = sbt("wq_b", [128, 4096], BF16)
    wk_b = sbt("wk_b", [128, 4096], BF16)
    wv_r = sbt("wv_r", [128, 4096], BF16)
    wo_r = sbt("wo_r", [128, 4096], BF16)
    cos_r = sbt("cos_r", [128, 8192], BF16)
    sin_r = sbt("sin_r", [128, 8192], BF16)
    stage = sbt("stage", [128, 2048], F32)
    stageB = sbt("stageB", [128, 2048], F32)
    x_sb = sbt("x_sb", [128, 4096], F32)
    out_sb = sbt("out_sb", [128, 4096], F32)
    xn_bf = sbt("xn_bf", [128, 2048], BF16)
    xnT_sb = sbt("xnT_sb", [128, 4096], BF16)
    qT_bf = sbt("qT_bf", [128, 1024], BF16)
    sqT_bf = sbt("sqT_bf", [128, 1024], BF16)
    kT_bf = sbt("kT_bf", [128, 1024], BF16)
    sk_bf = sbt("sk_bf", [128, 1024], BF16)
    v_bf = sbt("v_bf", [128, 1024], BF16)
    e1_bf = sbt("e1_bf", [128, 4096], BF16)
    pr_bf = sbt("pr_bf", [128, 4096], BF16)
    pT_sb = sbt("pT_sb", [128, 4096], BF16)
    amn_bf = sbt("amn_bf", [128, 512], BF16)
    amT_sb = sbt("amT_sb", [128, 512], BF16)
    attT_bf = sbt("attT_bf", [128, 512], BF16)
    scr = sbt("scr", [128, 4096], F32)
    memz_bf = sbt("memz_bf", [128, 520], BF16)
    g_tile = sbt("g_tile", [128, 512], BF16)
    gcol = sbt("gcol", [128, 4], F32)
    omg = sbt("omg", [128, 4], F32)
    brow = sbt("brow", [1, 512], F32)
    grow_bf = sbt("grow_bf", [1, 512], BF16)
    onesr_bf = sbt("onesr_bf", [1, 128], BF16)
    onesk_bf = sbt("onesk_bf", [128, 1], BF16)
    ident_sb = sbt("ident_sb", [128, 128], BF16)
    stat6 = sbt("stat6", [128, 24], F32)
    smv = sbt("smv", [128, 4], F32)
    lnv = sbt("lnv", [128, 2], F32)
    rstd = sbt("rstd", [128, 2], F32)
    negr = sbt("negr", [128, 2], F32)
    negmr = sbt("negmr", [128, 2], F32)
    negm = sbt("negm", [128, 16], F32)
    negms = sbt("negms", [128, 16], F32)
    den = sbt("den", [128, 16], F32)
    rdb = sbt("rdb", [128, 16], F32)
    rmb = sbt("rmb", [128, 8], F32)
    m2t = sbt("m2t", [128, 512], F32)
    cbias = sbt("cbias", [128, 3], F32)
    fscr = sbt("fscr", [128, 2048], BF16)
    magb = sbt("magb", [128, 2048], BF16)

    pb = [pst(f"pb{i}", [128, 512]) for i in range(8)]

    s_x = [sem("s_x0"), sem("s_x1")]; s_o = [sem("s_o0"), sem("s_o1")]; s_mcp = sem("s_mcp")
    s_wg = sem("s_wg"); s_bld = sem("s_bld")
    s_tb = [sem("s_tb0"), sem("s_tb1")]; s_ta = [sem("s_ta0"), sem("s_ta1")]
    s_ia = sem("s_ia"); s_iv = sem("s_iv"); s_ip = sem("s_ip"); s_ig = sem("s_ig")
    s_ln = sem("s_ln"); s_xa = sem("s_xa"); s_lnc = sem("s_lnc")
    s_xnt = sem("s_xnt"); s_xne = sem("s_xne")
    s_qt = sem("s_qt"); s_qte = sem("s_qte")
    s_mq = sem("s_mq"); s_eq = sem("s_eq"); s_sq = sem("s_sq")
    s_kt = sem("s_kt"); s_kte = sem("s_kte"); s_knt = sem("s_knt")
    s_mk = sem("s_mk"); s_ek = sem("s_ek"); s_sk = sem("s_sk")
    s_v = sem("s_v"); s_ve = sem("s_ve")
    s_sc = sem("s_sc"); s_mx = sem("s_mx"); s_e1 = sem("s_e1")
    s_pr = sem("s_pr"); s_pt = sem("s_pt"); s_pte = sem("s_pte")
    s_me = sem("s_me"); s_am = sem("s_am"); s_amn = sem("s_amn")
    s_amt = sem("s_amt"); s_ad = sem("s_ad"); s_bl = sem("s_bl")
    s_wo = sem("s_wo"); s_r1 = sem("s_r1")
    s_ff = sem("s_ff"); s_sqf = sem("s_sqf"); s_ss = sem("s_ss")
    s_mag = sem("s_mag"); s_r2 = sem("s_r2"); s_md = sem("s_md")

    blk = es.enter_context(nc.Block())
    with es:
        def ffn_mms(t_, tf, u):
            ppf = 2048 * (tf % 2)
            for mi, w_s in enumerate((cos_r, sin_r)):
                if tf == 0 and u == 0:
                    t_.wait_ge(s_ia, 6 if mi == 0 else 10)
                for half in range(2):
                    n = 8 * tf + 4 * u + 2 * mi + half
                    if n >= 2:
                        t_.wait_ge(s_sqf, n - 1)
                    for dt in range(NKT):
                        li = t_.matmul(pb[6 + half][:, :],
                                  xnT_sb[:, ppf + 256 * dt + 128 * u:
                                         ppf + 256 * dt + 128 * u + 128],
                                  w_s[:, 1024 * dt + 512 * half:
                                      1024 * dt + 512 * half + 512],
                                  start=(dt == 0), stop=(dt == NKT - 1))
                    li.then_inc(s_ff, 1)

        def ffn_evac(sc, tf, u):
            if tf >= 1 or u >= 1:
                sc.wait_ge(s_ss, 2 * tf + u)
            sc.drain()
            for mi in range(2):
                for half in range(2):
                    sc.wait_ge(s_ff, 8 * tf + 4 * u + 2 * mi + half + 1)
                    sc.activation(fscr[:, 1024 * mi + 512 * half:
                                       1024 * mi + 512 * half + 512],
                                  pb[6 + half][:, :],
                                  AF.Copy).then_inc(s_sqf, 1)

        def ffn_mag(sc, tf, u):
            if tf >= 1:
                sc.wait_ge(s_r2, 2 * (tf - 1) + u + 1)
            sc.wait_ge(s_ss, 2 * tf + u + 1)
            sc.drain()
            sc.activation(stage[:, 1024:2048], stage[:, 0:1024], AF.Ln,
                          bias=cbias[:, 2:3])
            sc.drain()
            sc.activation(magb[:, 1024 * u:1024 * u + 1024],
                          stage[:, 1024:2048], AF.Exp,
                          scale=0.5).then_inc(s_mag, 1)

        def ffn_dve(v, tf, u):
            ppf = 2048 * (tf % 2)
            v.wait_ge(s_mag, 2 * tf + u + 1)
            v.tensor_add(out_sb[:, ppf + 1024 * u:ppf + 1024 * u + 1024],
                         out_sb[:, ppf + 1024 * u:ppf + 1024 * u + 1024],
                         magb[:, 1024 * u:1024 * u + 1024]).then_inc(s_r2, 1)

        @blk.sync
        def _(s):
            s.dma_start(ident_sb[:, :], id_d[:, :]).then_inc(s_bld, 16)
            s.dma_start(brow[:, :], bet_d[:, :]).then_inc(s_bld, 16)
            for t in range(min(2, nch)):
                src = x_d[256 * t:256 * (t + 1), :].rearrange(
                    "(u p) d -> p u d", u=2)
                dstx = x_sb[:, 2048 * (t % 2):2048 * (t % 2) + 2048].rearrange(
                    "p (u d) -> p u d", u=2)
                s.dma_start(dstx, src).then_inc(s_x[t % 2], 16)
            for t in range(nch):
                if t >= 1:
                    s.wait_ge(s_me, t)
                    s.dma_start(memz_bf[64:128, :],
                                memz_bf[0:64, :]).then_inc(s_mcp, 16)
                if t + 2 < nch:
                    s.wait_ge(s_r1, 2 * t + 2)
                    tt = t + 2
                    src = x_d[256 * tt:256 * (tt + 1), :].rearrange(
                        "(u p) d -> p u d", u=2)
                    dstx = x_sb[:, 2048 * (tt % 2):
                                2048 * (tt % 2) + 2048].rearrange(
                        "p (u d) -> p u d", u=2)
                    s.dma_start(dstx, src).then_inc(s_x[t % 2], 16)
                if t >= 1:
                    s.wait_ge(s_r2, 2 * t)
                    tt = t - 1
                    dst = y_d[256 * tt:256 * (tt + 1), :].rearrange(
                        "(u p) d -> p u d", u=2)
                    srco = out_sb[:, 2048 * (tt % 2):
                                  2048 * (tt % 2) + 2048].rearrange(
                        "p (u d) -> p u d", u=2)
                    s.dma_start(dst, srco).then_inc(s_o[tt % 2], 16)
            s.wait_ge(s_r2, 2 * nch)
            tt = nch - 1
            dst = y_d[256 * tt:256 * (tt + 1), :].rearrange(
                "(u p) d -> p u d", u=2)
            srco = out_sb[:, 2048 * (tt % 2):2048 * (tt % 2) + 2048].rearrange(
                "p (u d) -> p u d", u=2)
            s.dma_start(dst, srco).then_inc(s_o[tt % 2], 16)
            s.wait_ge(s_o[0], 16 * ((nch + 1) // 2))
            s.wait_ge(s_o[1], 16 * (nch // 2))

        @blk.gpsimd
        def _(g):
            g.memset(memz_bf[:, :], 0.0)
            g.drain()
            g.memset(memz_bf[:, :].rearrange("p (h c) -> p h c", c=65)
                     [:, :, 64:65], 1.0 / 64.0)
            g.memset(onesr_bf[:, :], 1.0)
            g.memset(cbias[:, 0:1], PI_2)
            g.memset(cbias[:, 1:2], 1e-5)
            g.memset(cbias[:, 2:3], 1e-38)
            g.memset(onesk_bf[:, :], 1.0)
            g.drain().then_inc(s_ig, 1)
            for w_d, w_s in ((wq_d, wq_b), (wk_d, wk_b), (wv_d, wv_r),
                             (wo_d, wo_r)):
                g.dma_start(w_s[:, :].rearrange("p (k m) -> p k m",
                                                 k=w_d.shape[0] // 128),
                            w_d[:, :].rearrange("(k p) m -> p k m",
                                                p=128)).then_inc(s_wg, 16)
            srcs = []
            for a_d, n_d in ((th_d, thn_d), (ph_d, phn_d)):
                for p in range(NTH):
                    srcs.append((
                        a_d[256 * p:256 * (p + 1), :].rearrange(
                            "(gg pp) d -> pp gg d", pp=128),
                        n_d[256 * p:256 * (p + 1), :].rearrange(
                            "(gg pp) d -> pp gg d", pp=128)))
            stg3 = [stage[:, :].rearrange("p (gg d) -> p gg d", gg=2),
                    stageB[:, :].rearrange("p (gg d) -> p gg d", gg=2)]

            def emit_accum(m):
                g.wait_ge(s_tb[m % 2], 16 * (m // 2 + 1))
                g.dma_start(stg3[m % 2], srcs[m][1],
                            accum_op=ALU.add).then_inc(s_ta[m % 2], 16)

            for n in range(2 * NTH):
                if n >= 2:
                    g.wait_ge(s_ia, n + 1)
                g.dma_start(stg3[n % 2], srcs[n][0]).then_inc(s_tb[n % 2], 16)
                if n >= 1:
                    emit_accum(n - 1)
            emit_accum(2 * NTH - 1)
            for t in range(nch):
                for u in range(2):
                    if t == 0 and u == 0:
                        g.wait_ge(s_ia, 10)
                    if 2 * t + u >= 1:
                        g.wait_ge(s_mag, 2 * t + u)
                    g.wait_ge(s_sqf, 8 * t + 4 * u + 2)
                    g.tensor_mul(stage[:, 0:1024], fscr[:, 0:1024],
                                 fscr[:, 0:1024])
                    g.wait_ge(s_sqf, 8 * t + 4 * u + 4)
                    g.drain()
                    g.tensor_mul(stage[:, 1024:2048], fscr[:, 1024:2048],
                                 fscr[:, 1024:2048])
                    g.drain()
                    g.tensor_add(stage[:, 0:1024], stage[:, 0:1024],
                                 stage[:, 1024:2048]).then_inc(s_ss, 1)

        @blk.scalar
        def _(sc):
            sc.wait_ge(s_bld, 32)
            sc.activation(grow_bf[:, :], brow[:, :], AF.Sigmoid).then_inc(s_ia, 1)
            sc.drain().then_inc(s_ia, 1)

            def emit_sin(n):
                i, p = divmod(n, NTH)
                dst = cos_r if i == 0 else sin_r
                buf = stage if n % 2 == 0 else stageB
                sc.wait_ge(s_ta[n % 2], 16 * (n // 2 + 1))
                if n == 0:
                    sc.wait_ge(s_ig, 1)
                if i == 0:
                    sc.activation(dst[:, 2048 * p:2048 * (p + 1)],
                                  buf[:, :], AF.Sin,
                                  bias=cbias[:, 0:1]).then_inc(s_ia, 1)
                else:
                    sc.activation(dst[:, 2048 * p:2048 * (p + 1)],
                                  buf[:, :], AF.Sin).then_inc(s_ia, 1)

            smv3 = smv[:, :].rearrange("p (u c) -> p u c", c=2)

            def ln_chain(tt):
                sc.wait_ge(s_ln, tt + 1)
                if tt >= 1:
                    sc.wait_ge(s_xa, 2 * tt)
                sc.activation(lnv[:, :], smv3[:, :, 1:2], AF.Ln, bias=cbias[:, 1:2])
                sc.drain()
                sc.activation(rstd[:, :], lnv[:, :], AF.Exp, scale=-0.5)
                sc.drain()
                sc.activation(negr[:, :], rstd[:, :], AF.Copy, scale=-1.0)
                sc.drain()
                for u in range(2):
                    sc.activation(negmr[:, u:u + 1], smv[:, 2 * u:2 * u + 1],
                                  AF.Copy, scale=negr[:, u:u + 1])
                sc.drain().then_inc(s_lnc, 1)

            ln_chain(0)
            for n in range(2 * NTH):
                emit_sin(n)
            for t in range(nch):
                pp = 2048 * (t % 2)
                sc.wait_ge(s_mq, t + 1)
                sc.activation(scr[:, 1024:2048], scr[:, 0:1024],
                              AF.Exp).then_inc(s_eq, 1)
                sc.wait_ge(s_mk, t + 1)
                sc.activation(scr[:, 3072:4096], scr[:, 2048:3072],
                              AF.Exp).then_inc(s_ek, 1)
                for u in range(2):
                    sc.wait_ge(s_mx, 2 * t + u + 1)
                    for h in range(8):
                        jb = 2 * (h % 2) + (h // 4)
                        ib = (h // 2) % 2
                        c = 8 * u + 2 * jb + ib
                        sl = pb[2 + jb][:, 256 * ib:256 * ib + 256]
                        sc.activation(e1_bf[:, 2048 * u + 256 * h:
                                            2048 * u + 256 * h + 256],
                                      sl, AF.Exp, bias=negms[:, c:c + 1],
                                      scale=0.125, accum_out=den[:, c:c + 1])
                    sc.drain().then_inc(s_e1, 1)
                    if t >= 1:
                        ffn_evac(sc, t - 1, u)
                        if u == 1:
                            ffn_mag(sc, t - 1, 0)
                            ffn_mag(sc, t - 1, 1)
                    if u == 0 and t + 1 < nch:
                        ln_chain(t + 1)
            for u in range(2):
                ffn_evac(sc, nch - 1, u)
            ffn_mag(sc, nch - 1, 0)
            ffn_mag(sc, nch - 1, 1)

        @blk.tensor
        def _(t_):
            t_.wait_ge(s_ia, 2)
            t_.wait_ge(s_ig, 1)
            t_.wait_ge(s_bld, 32)
            t_.matmul(pb[2][:, :], onesr_bf[0:1, :], grow_bf[0:1, :],
                      start=True, stop=True).then_inc(s_ip, 1)
            for ft in range(4):
                t_.matmul(pb[3][:, ft:ft + 1],
                          grow_bf[0:1, 128 * ft:128 * (ft + 1)],
                          onesr_bf[0:1, 0:1], start=True,
                          stop=True).then_inc(s_ip, 1)
            t_.wait_ge(s_iv, 2)
            for t in range(nch):
                pp = 2048 * (t % 2)
                t_.wait_ge(s_xa, 2 * t + 2)
                if t >= 1:
                    t_.wait_ge(s_amn, 2 * t)
                    t_.wait_ge(s_bl, 2 * t)
                for dt in range(NKT):
                    if dt == 4 and t >= 1:
                        t_.wait_ge(s_r1, 2 * t)
                    for u in range(2):
                        li = t_.matmul(pb[2 + dt // 2][:, 128 * (2 * (dt % 2) + u):
                                                  128 * (2 * (dt % 2) + u) + 128],
                                  xn_bf[:, 1024 * u + 128 * dt:
                                        1024 * u + 128 * dt + 128],
                                  ident_sb[:, :], start=True, stop=True)
                li.then_inc(s_xnt, 1)
                t_.wait_ge(s_xne, t + 1)
                if t == 0:
                    t_.wait_ge(s_wg, 64)
                for mt in range(4):
                    for kt in range(NKT):
                        li = t_.matmul(pb[2 + mt // 2][:, 256 * (mt % 2):
                                                  256 * (mt % 2) + 256],
                                  wq_b[:, 512 * kt + 128 * mt:
                                       512 * kt + 128 * mt + 128],
                                  xnT_sb[:, pp + 256 * kt:pp + 256 * kt + 256],
                                  start=(kt == 0), stop=(kt == NKT - 1))
                li.then_inc(s_qt, 1)
                if t == 0:
                    t_.wait_ge(s_wg, 64)
                for mt in range(4):
                    for kt in range(NKT):
                        li = t_.matmul(pb[4 + mt // 2][:, 256 * (mt % 2):
                                                  256 * (mt % 2) + 256],
                                  wk_b[:, 512 * kt + 128 * mt:
                                       512 * kt + 128 * mt + 128],
                                  xnT_sb[:, pp + 256 * kt:pp + 256 * kt + 256],
                                  start=(kt == 0), stop=(kt == NKT - 1))
                li.then_inc(s_kt, 1)
                t_.wait_ge(s_qte, t + 1)
                if t == 0:
                    t_.wait_ge(s_wg, 64)
                for u in range(2):
                    for kt in range(NKT):
                        li = t_.matmul(pb[2 + u][:, :],
                                  xnT_sb[:, pp + 256 * kt + 128 * u:
                                         pp + 256 * kt + 128 * u + 128],
                                  wv_r[:, 512 * kt:512 * kt + 512],
                                  start=(kt == 0), stop=(kt == NKT - 1))
                    li.then_inc(s_v, 1)
                t_.wait_ge(s_kte, t + 1)
                for u in range(2):
                    for ft in range(4):
                        li = t_.matmul(pb[4 + u][:, 128 * ft:128 * ft + 128],
                                  kT_bf[:, 256 * ft + 128 * u:
                                        256 * ft + 128 * u + 128],
                                  ident_sb[:, :], start=True, stop=True)
                li.then_inc(s_knt, 1)
                for u in range(2):
                    if u == 1:
                        t_.wait_ge(s_pte, 2 * t + 1)
                    for h in (0, 2, 4, 6, 1, 3, 5, 7):
                        if u == 0:
                            if h == 0:
                                t_.wait_ge(s_ve, 2 * t + 1)
                            elif h == 4:
                                t_.wait_ge(s_ve, 2 * t + 2)
                            elif h == 1:
                                t_.wait_ge(s_sk, t + 1)
                        b = 64 * (h % 2)
                        jb = 2 * (h % 2) + (h // 4)
                        ib = (h // 2) % 2
                        li = t_.matmul(pb[2 + jb][:, 256 * ib:256 * ib + 256],
                                  qT_bf[b:b + 64, 256 * (h // 2) + 128 * u:
                                        256 * (h // 2) + 128 * u + 128],
                                  kT_bf[b:b + 64, 256 * (h // 2):
                                        256 * (h // 2) + 256],
                                  start=True, stop=True)
                    li.then_inc(s_sc, 1)
                    if t >= 1:
                        ffn_mms(t_, t - 1, u)
                    t_.wait_ge(s_pr, 2 * t + u + 1)
                    for h in range(8):
                        for kt in range(2):
                            li = t_.matmul(pb[2 + h // 2][:, 256 * (h % 2) + 128 * kt:
                                                     256 * (h % 2) + 128 * kt + 128],
                                      pr_bf[:, 2048 * u + 256 * h + 128 * kt:
                                            2048 * u + 256 * h + 128 * kt + 128],
                                      ident_sb[:, :], start=True, stop=True)
                    li.then_inc(s_pt, 1)
                for u in range(2):
                    bank_ad = pb[2] if u == 0 else pb[4]
                    t_.wait_ge(s_pte, 2 * t + 2)
                    if u == 1:
                        t_.wait_ge(s_bl, 2 * t + 1)
                        t_.wait_ge(s_amn, 2 * t + 1)
                        t_.wait_ge(s_r1, 2 * t + 1)
                    for kt in range(2):
                        for h in range(8):
                            b = 64 * (h % 2)
                            li = t_.matmul(bank_ad[b:b + 64, 128 * (h // 2):
                                              128 * (h // 2) + 128],
                                      v_bf[:, 512 * kt + 64 * h:
                                           512 * kt + 64 * h + 64],
                                      pT_sb[:, 2048 * u + 1024 * kt + 128 * h:
                                            2048 * u + 1024 * kt + 128 * h + 128],
                                      start=(kt == 0 and h <= 1),
                                      stop=(kt == 1 and h >= 6),
                                      skip_group_check=True)
                    li.then_inc(s_ad, 1)
                    if t >= 1:
                        t_.wait_ge(s_me, t)
                        t_.wait_ge(s_mcp, 16 * t)
                    else:
                        t_.wait_ge(s_ig, 1)
                    t_.wait_ge(s_sq, t + 1)
                    bkA = pb[3] if u == 0 else pb[5]
                    bkB = pb[4] if u == 0 else pb[2]
                    for h in range(8):
                        bank = bkA if h % 2 == 0 else bkB
                        col = 65 * (h // 2)
                        b = 64 * (h % 2)
                        li = t_.matmul(bank[:, col:col + 65],
                                  sqT_bf[b:b + 64, 256 * (h // 2) + 128 * u:
                                         256 * (h // 2) + 128 * u + 128],
                                  memz_bf[b:b + 64, 65 * h:65 * h + 65],
                                  start=True, stop=True)
                    li.then_inc(s_am, 1)
                    bank_mt = pb[5] if u == 0 else pb[3]
                    t_.wait_ge(s_amn, 2 * t + u + 1)
                    for ft in range(4):
                        li = t_.matmul(bank_mt[:, 128 * ft:128 * ft + 128],
                                  amn_bf[:, 128 * ft:128 * ft + 128],
                                  ident_sb[:, :], start=True, stop=True)
                    li.then_inc(s_amt, 1)
                    t_.wait_ge(s_bl, 2 * t + u + 1)
                    if t == 0 and u == 0:
                        t_.wait_ge(s_wg, 64)
                    bh0 = pb[2] if u == 0 else pb[4]
                    bh1 = pb[3] if u == 0 else pb[5]
                    for half in range(2):
                        for ft in range(4):
                            li = t_.matmul((bh0 if half == 0 else bh1)[:, :],
                                      attT_bf[:, 128 * ft:128 * ft + 128],
                                      wo_r[:, 1024 * ft + 512 * half:
                                           1024 * ft + 512 * half + 512],
                                      start=(ft == 0), stop=(ft == 3))
                    li.then_inc(s_wo, 1)
                if t >= 1:
                    t_.wait_ge(s_me, t)
                for kt in range(2):
                    for h in range(8):
                        t_.matmul(pb[0][0:64, 64 * h:64 * h + 64],
                                  sk_bf[:, 512 * kt + 64 * h:
                                        512 * kt + 64 * h + 64],
                                  v_bf[:, 512 * kt + 64 * h:
                                       512 * kt + 64 * h + 64],
                                  start=(t == 0 and kt == 0 and h == 0),
                                  stop=(t == nch - 1 and kt == 1 and h == 7),
                                  skip_group_check=True)
                        li = t_.matmul(pb[1][0:64, h:h + 1],
                                  sk_bf[:, 512 * kt + 64 * h:
                                        512 * kt + 64 * h + 64],
                                  onesk_bf[:, :],
                                  start=(t == 0 and kt == 0 and h == 0),
                                  stop=(t == nch - 1 and kt == 1 and h == 7),
                                  skip_group_check=True)
                li.then_inc(s_md, 1)
            for u in range(2):
                ffn_mms(t_, nch - 1, u)

        @blk.vector
        def _(v):
            v.wait_ge(s_ip, 1)
            v.tensor_copy(g_tile[:, :], pb[2][:, :]).then_inc(s_iv, 1)
            v.wait_ge(s_ip, 5)
            v.tensor_copy(gcol[:, :], pb[3][:, 0:4])
            v.drain()
            v.tensor_scalar(omg[:, :], gcol[:, :], -1.0, 1.0, ALU.mult, ALU.add)
            v.drain().then_inc(s_iv, 1)
            def bn_xn_stats(tt):
                pq = 2048 * (tt % 2)
                v.wait_ge(s_x[tt % 2], 16 * (tt // 2 + 1))
                if tt >= 1:
                    v.wait_ge(s_lnc, tt)
                for u in range(2):
                    v.bn_stats(stat6[:, 12 * u:12 * u + 6],
                               x_sb[:, pq + 1024 * u:pq + 1024 * u + 512])
                    v.bn_stats(stat6[:, 12 * u + 6:12 * u + 12],
                               x_sb[:, pq + 1024 * u + 512:
                                    pq + 1024 * u + 1024])
                    v.drain()
                    li = v.bn_aggr(smv[:, 2 * u:2 * u + 2],
                                   stat6[:, 12 * u:12 * u + 12])
                li.then_inc(s_ln, 1)

            def bn_xn_affine(tt):
                pq = 2048 * (tt % 2)
                v.wait_ge(s_lnc, tt + 1)
                if tt >= 1:
                    v.wait_ge(s_xnt, tt)
                for u in range(2):
                    v.tensor_scalar(xn_bf[:, 1024 * u:1024 * (u + 1)],
                                    x_sb[:, pq + 1024 * u:pq + 1024 * (u + 1)],
                                    rstd[:, u:u + 1], negmr[:, u:u + 1],
                                    ALU.mult, ALU.add).then_inc(s_xa, 1)

            bn_xn_stats(0)
            bn_xn_affine(0)
            for t in range(nch):
                pp = 2048 * (t % 2)
                v.wait_ge(s_xnt, t + 1)
                if t >= 2:
                    v.wait_ge(s_ff, 8 * (t - 2) + 8)
                for j in range(4):
                    li = v.tensor_copy(xnT_sb[:, pp + 512 * j:pp + 512 * j + 512],
                                       pb[2 + j][:, :])
                li.then_inc(s_xne, 1)
                v.wait_ge(s_qt, t + 1)
                v.tensor_copy(qT_bf[:, 0:512], pb[2][:, :])
                v.tensor_copy(qT_bf[:, 512:1024],
                              pb[3][:, :]).then_inc(s_qte, 1)
                v.drain()
                v.tensor_scalar_min(scr[:, 0:1024], qT_bf[:, :],
                                    0.0).then_inc(s_mq, 1)
                v.wait_ge(s_kt, t + 1)
                v.tensor_copy(kT_bf[:, 0:512], pb[4][:, :])
                v.tensor_copy(kT_bf[:, 512:1024],
                              pb[5][:, :]).then_inc(s_kte, 1)
                v.wait_ge(s_eq, t + 1)
                v.scalar_tensor_tensor(sqT_bf[:, :], qT_bf[:, :], 0.0,
                                       scr[:, 1024:2048], ALU.max,
                                       ALU.add).then_inc(s_sq, 1)
                for u in range(2):
                    v.wait_ge(s_v, 2 * t + u + 1)
                    v.tensor_copy(v_bf[:, 512 * u:512 * u + 512],
                                  pb[2 + u][:, :]).then_inc(s_ve, 1)
                v.wait_ge(s_knt, t + 1)
                for u in range(2):
                    li = v.tensor_scalar_min(scr[:, 2048 + 512 * u:
                                                 2048 + 512 * u + 512],
                                             pb[4 + u][:, :], 0.0)
                li.then_inc(s_mk, 1)
                v.wait_ge(s_ek, t + 1)
                for u in range(2):
                    li = v.scalar_tensor_tensor(sk_bf[:, 512 * u:512 * u + 512],
                                                pb[4 + u][:, :], 0.0,
                                                scr[:, 3072 + 512 * u:
                                                    3072 + 512 * u + 512],
                                                ALU.max, ALU.add)
                li.then_inc(s_sk, 1)
                if t >= 1:
                    v.wait_ge(s_md, t)
                    mz3 = memz_bf[0:64, :].rearrange("p (h c) -> p h c", c=65)
                    v.tensor_copy(mz3[:, :, 0:64],
                                  pb[0][0:64, :].rearrange("p (h c) -> p h c",
                                                           c=64))
                    v.tensor_scalar_add(mz3[:, :, 64:65],
                                        pb[1][0:64, 0:8].unsqueeze(2),
                                        1.0 / 64.0).then_inc(s_me, 1)
                for u in range(2):
                    v.wait_ge(s_sc, 2 * t + u + 1)
                    for j in range(4):
                        v.reduce_max(negm[:, 8 * u + 2 * j:8 * u + 2 * j + 2],
                                     pb[2 + j][:, :].rearrange(
                                         "p (i c) -> p i c", c=256),
                                     axis=AX.X, negate=True)
                    v.drain()
                    v.tensor_scalar_mul(negms[:, 8 * u:8 * u + 8],
                                        negm[:, 8 * u:8 * u + 8],
                                        0.125).then_inc(s_mx, 1)
                    if u == 0 and t + 1 < nch:
                        bn_xn_stats(t + 1)
                    v.wait_ge(s_e1, 2 * t + u + 1)
                    v.reciprocal(rdb[:, 8 * u:8 * u + 8],
                                 den[:, 8 * u:8 * u + 8])
                    v.drain()
                    for h in range(8):
                        jb = 2 * (h % 2) + (h // 4)
                        ib = (h // 2) % 2
                        c = 8 * u + 2 * jb + ib
                        li = v.tensor_scalar_mul(
                            pr_bf[:, 2048 * u + 256 * h:2048 * u + 256 * h + 256],
                            e1_bf[:, 2048 * u + 256 * h:2048 * u + 256 * h + 256],
                            rdb[:, c:c + 1])
                    li.then_inc(s_pr, 1)
                    v.wait_ge(s_pt, 2 * t + u + 1)
                    for j in range(4):
                        src = pb[2 + j][:, :].rearrange(
                            "p (i kt q) -> p i kt q", i=2, kt=2)
                        dst = pT_sb[:, 2048 * u:2048 * u + 2048].rearrange(
                            "p (kt h q) -> p h kt q", kt=2, h=8)[:, 2 * j:2 * j + 2]
                        li = v.tensor_copy(dst, src)
                    li.then_inc(s_pte, 1)
                    if u == 1 and t + 1 < nch:
                        bn_xn_affine(t + 1)
                for u in range(2):
                    bkA = pb[3] if u == 0 else pb[5]
                    bkB = pb[4] if u == 0 else pb[2]
                    bank_mt = pb[5] if u == 0 else pb[3]
                    bank_ad = pb[2] if u == 0 else pb[4]
                    v.wait_ge(s_am, 2 * t + u + 1)
                    a3 = bkA[:, 0:260].rearrange("p (j c) -> p j c", c=65)
                    b3 = bkB[:, 0:260].rearrange("p (j c) -> p j c", c=65)
                    rm3 = rmb[:, 0:8].rearrange("p (j i) -> p j i", i=2)
                    v.reciprocal(rm3[:, :, 0:1], a3[:, :, 64:65])
                    v.reciprocal(rm3[:, :, 1:2], b3[:, :, 64:65])
                    v.drain()
                    v.tensor_tensor(m2t[:, :],
                                    rmb[:, 0:8].unsqueeze(2).broadcast_to(
                                        [128, 8, 64]),
                                    g_tile[:, :].rearrange("p (h c) -> p h c",
                                                           c=64), ALU.mult)
                    v.drain()
                    amn4 = amn_bf[:, :].rearrange("p (j i c) -> p j i c",
                                                  i=2, c=64)
                    m24 = m2t[:, :].rearrange("p (j i c) -> p j i c",
                                              i=2, c=64)
                    v.tensor_tensor(amn4[:, :, 0:1, :],
                                    a3[:, :, 0:64].unsqueeze(2),
                                    m24[:, :, 0:1, :], ALU.mult)
                    v.tensor_tensor(amn4[:, :, 1:2, :],
                                    b3[:, :, 0:64].unsqueeze(2),
                                    m24[:, :, 1:2, :],
                                    ALU.mult).then_inc(s_amn, 1)
                    v.wait_ge(s_amt, 2 * t + u + 1)
                    v.tensor_copy(amT_sb[:, :], bank_mt[:, :])
                    v.drain()
                    v.wait_ge(s_ad, 2 * t + u + 1)
                    for ft in range(4):
                        li = v.scalar_tensor_tensor(
                            attT_bf[:, 128 * ft:128 * ft + 128],
                            bank_ad[:, 128 * ft:128 * ft + 128],
                            omg[:, ft:ft + 1],
                            amT_sb[:, 128 * ft:128 * ft + 128],
                            ALU.mult, ALU.add)
                    li.then_inc(s_bl, 1)
                    v.wait_ge(s_wo, 2 * t + u + 1)
                    if t >= 2 and u == 0:
                        v.wait_ge(s_o[t % 2], 16 * (t // 2))
                    bh0 = pb[2] if u == 0 else pb[4]
                    bh1 = pb[3] if u == 0 else pb[5]
                    v.tensor_add(out_sb[:, pp + 1024 * u:pp + 1024 * u + 512],
                                 x_sb[:, pp + 1024 * u:pp + 1024 * u + 512],
                                 bh0[:, :])
                    v.tensor_add(out_sb[:, pp + 1024 * u + 512:
                                        pp + 1024 * u + 1024],
                                 x_sb[:, pp + 1024 * u + 512:
                                      pp + 1024 * u + 1024],
                                 bh1[:, :]).then_inc(s_r1, 1)
                if u == 1 and t >= 1:
                    ffn_dve(v, t - 1, 0)
                    ffn_dve(v, t - 1, 1)
            for u in range(2):
                ffn_dve(v, nch - 1, u)

    return nc


def kernel(L0_attn_x, L0_attn_Wq, L0_attn_Wk, L0_attn_Wv, L0_attn_betas,
           L0_attn_Wo, L0_ffn_theta, L0_ffn_phi, L0_ffn_theta_noise,
           L0_ffn_phi_noise):
    x = np.asarray(L0_attn_x)
    nc = build_nc()
    common = {
        "wq": np.ascontiguousarray(L0_attn_Wq),
        "wk": np.ascontiguousarray(L0_attn_Wk),
        "wv": np.ascontiguousarray(L0_attn_Wv),
        "wo": np.ascontiguousarray(L0_attn_Wo),
        "betas": np.ascontiguousarray(np.asarray(L0_attn_betas).reshape(1, 512)),
        "th": np.ascontiguousarray(L0_ffn_theta),
        "thn": np.ascontiguousarray(L0_ffn_theta_noise),
        "ph": np.ascontiguousarray(L0_ffn_phi),
        "phn": np.ascontiguousarray(L0_ffn_phi_noise),
    }
    in_maps = [dict(common, x=np.ascontiguousarray(x[b])) for b in range(NCORE)]
    res = run_bass_kernel_spmd(nc, in_maps, core_ids=list(range(NCORE)))
    out = np.stack([np.asarray(res.results[b]["y"]) for b in range(NCORE)],
                   axis=0)
    return out


# version 16
# speedup vs baseline: 1.0404x; 1.0383x over previous
import contextlib
import numpy as np
import ml_dtypes
import concourse.bass as bass
import concourse.mybir as mybir
from concourse.bass_utils import run_bass_kernel_spmd

F32 = mybir.dt.float32
BF16 = mybir.dt.bfloat16
AF = mybir.ActivationFunctionType
ALU = mybir.AluOpType
AX = mybir.AxisListType

NCORE = 8
S, D = 4096, 1024
NKT = D // 128
PI_2 = float(np.pi / 2)
NTH = 4


def build_nc(nch=16):
    nc = bass.Bass("TRN2", target_bir_lowering=False)
    x_d = nc.dram_tensor("x", [S, D], F32, kind="ExternalInput").ap()
    wq_d = nc.dram_tensor("wq", [D, 512], F32, kind="ExternalInput").ap()
    wk_d = nc.dram_tensor("wk", [D, 512], F32, kind="ExternalInput").ap()
    wv_d = nc.dram_tensor("wv", [D, 512], F32, kind="ExternalInput").ap()
    wo_d = nc.dram_tensor("wo", [512, D], F32, kind="ExternalInput").ap()
    bet_d = nc.dram_tensor("betas", [1, 512], F32, kind="ExternalInput").ap()
    th_d = nc.dram_tensor("th", [D, D], F32, kind="ExternalInput").ap()
    thn_d = nc.dram_tensor("thn", [D, D], F32, kind="ExternalInput").ap()
    ph_d = nc.dram_tensor("ph", [D, D], F32, kind="ExternalInput").ap()
    phn_d = nc.dram_tensor("phn", [D, D], F32, kind="ExternalInput").ap()
    y_d = nc.dram_tensor("y", [S, D], F32, kind="ExternalOutput").ap()
    id_d = nc.inline_tensor(np.eye(128, dtype=ml_dtypes.bfloat16), "identd").ap()

    es = contextlib.ExitStack()
    sbt = lambda name, shape, dt: es.enter_context(nc.sbuf_tensor(name, shape, dt))
    pst = lambda name, shape: es.enter_context(nc.psum_tensor(name, shape, F32))
    sem = lambda name: es.enter_context(nc.semaphore(name))

    wq_b = sbt("wq_b", [128, 4096], BF16)
    wk_b = sbt("wk_b", [128, 4096], BF16)
    wv_r = sbt("wv_r", [128, 4096], BF16)
    wo_r = sbt("wo_r", [128, 4096], BF16)
    cos_r = sbt("cos_r", [128, 8192], BF16)
    sin_r = sbt("sin_r", [128, 8192], BF16)
    stage = sbt("stage", [128, 2048], F32)
    stageB = sbt("stageB", [128, 2048], F32)
    x_sb = sbt("x_sb", [128, 4096], F32)
    out_sb = sbt("out_sb", [128, 4096], F32)
    xn_bf = sbt("xn_bf", [128, 2048], BF16)
    xnT_sb = sbt("xnT_sb", [128, 4096], BF16)
    qT_bf = sbt("qT_bf", [128, 1024], BF16)
    sqT_bf = sbt("sqT_bf", [128, 1024], BF16)
    kT_bf = sbt("kT_bf", [128, 1024], BF16)
    sk_bf = sbt("sk_bf", [128, 1024], BF16)
    v_bf = sbt("v_bf", [128, 1024], BF16)
    e1_bf = sbt("e1_bf", [128, 4096], BF16)
    pr_bf = sbt("pr_bf", [128, 4096], BF16)
    pT_sb = sbt("pT_sb", [128, 4096], BF16)
    amn_bf = sbt("amn_bf", [128, 512], BF16)
    amT_sb = sbt("amT_sb", [128, 512], BF16)
    attT_bf = sbt("attT_bf", [128, 512], BF16)
    scr = sbt("scr", [128, 4096], F32)
    memz_bf = sbt("memz_bf", [128, 520], BF16)
    g_tile = sbt("g_tile", [128, 512], BF16)
    gcol = sbt("gcol", [128, 4], F32)
    omg = sbt("omg", [128, 4], F32)
    brow = sbt("brow", [1, 512], F32)
    grow_bf = sbt("grow_bf", [1, 512], BF16)
    onesr_bf = sbt("onesr_bf", [1, 128], BF16)
    onesk_bf = sbt("onesk_bf", [128, 1], BF16)
    ident_sb = sbt("ident_sb", [128, 128], BF16)
    stat6 = sbt("stat6", [128, 24], F32)
    smv = sbt("smv", [128, 4], F32)
    lnv = sbt("lnv", [128, 2], F32)
    rstd = sbt("rstd", [128, 2], F32)
    negr = sbt("negr", [128, 2], F32)
    negmr = sbt("negmr", [128, 2], F32)
    negm = sbt("negm", [128, 16], F32)
    negms = sbt("negms", [128, 16], F32)
    den = sbt("den", [128, 16], F32)
    rdb = sbt("rdb", [128, 16], F32)
    rmb = sbt("rmb", [128, 8], F32)
    m2t = sbt("m2t", [128, 512], F32)
    cbias = sbt("cbias", [128, 3], F32)
    fscr = sbt("fscr", [128, 2048], BF16)
    magb = sbt("magb", [128, 2048], BF16)

    pb = [pst(f"pb{i}", [128, 512]) for i in range(8)]

    s_x = [sem("s_x0"), sem("s_x1")]; s_o = [sem("s_o0"), sem("s_o1")]; s_mcp = sem("s_mcp")
    s_wg = sem("s_wg"); s_bld = sem("s_bld")
    s_tb = [sem("s_tb0"), sem("s_tb1")]; s_ta = [sem("s_ta0"), sem("s_ta1")]
    s_ia = sem("s_ia"); s_iv = sem("s_iv"); s_ip = sem("s_ip"); s_ig = sem("s_ig")
    s_ln = sem("s_ln"); s_xa = sem("s_xa"); s_lnc = sem("s_lnc")
    s_xnt = sem("s_xnt"); s_xne = sem("s_xne")
    s_qt = sem("s_qt"); s_qte = sem("s_qte")
    s_mq = sem("s_mq"); s_eq = sem("s_eq"); s_sq = sem("s_sq")
    s_kt = sem("s_kt"); s_kte = sem("s_kte"); s_knt = sem("s_knt")
    s_mk = sem("s_mk"); s_ek = sem("s_ek"); s_sk = sem("s_sk")
    s_v = sem("s_v"); s_ve = sem("s_ve")
    s_sc = sem("s_sc"); s_mx = sem("s_mx"); s_e1 = sem("s_e1")
    s_pr = sem("s_pr"); s_pt = sem("s_pt"); s_pte = sem("s_pte")
    s_me = sem("s_me"); s_am = sem("s_am"); s_amn = sem("s_amn")
    s_amt = sem("s_amt"); s_ad = sem("s_ad"); s_bl = sem("s_bl")
    s_wo = sem("s_wo"); s_r1 = sem("s_r1")
    s_ff = sem("s_ff"); s_sqf = sem("s_sqf"); s_ss = sem("s_ss")
    s_mag = sem("s_mag"); s_r2 = sem("s_r2"); s_md = sem("s_md")

    blk = es.enter_context(nc.Block())
    with es:
        def ffn_mms(t_, tf, u):
            ppf = 2048 * (tf % 2)
            for mi, w_s in enumerate((cos_r, sin_r)):
                if tf == 0 and u == 0:
                    t_.wait_ge(s_ia, 6 if mi == 0 else 10)
                for half in range(2):
                    n = 8 * tf + 4 * u + 2 * mi + half
                    if n >= 2:
                        t_.wait_ge(s_sqf, n - 1)
                    for dt in range(NKT):
                        li = t_.matmul(pb[6 + half][:, :],
                                  xnT_sb[:, ppf + 256 * dt + 128 * u:
                                         ppf + 256 * dt + 128 * u + 128],
                                  w_s[:, 1024 * dt + 512 * half:
                                      1024 * dt + 512 * half + 512],
                                  start=(dt == 0), stop=(dt == NKT - 1))
                    li.then_inc(s_ff, 1)

        def ffn_evac(sc, tf, u):
            if tf >= 1 or u >= 1:
                sc.wait_ge(s_ss, 2 * tf + u)
            sc.drain()
            for mi in range(2):
                for half in range(2):
                    sc.wait_ge(s_ff, 8 * tf + 4 * u + 2 * mi + half + 1)
                    sc.activation(fscr[:, 1024 * mi + 512 * half:
                                       1024 * mi + 512 * half + 512],
                                  pb[6 + half][:, :],
                                  AF.Copy).then_inc(s_sqf, 1)

        def ffn_mag(sc, tf, u):
            if tf >= 1:
                sc.wait_ge(s_r2, 2 * (tf - 1) + u + 1)
            sc.wait_ge(s_ss, 2 * tf + u + 1)
            sc.drain()
            sc.activation(stage[:, 1024:2048], stage[:, 0:1024], AF.Ln,
                          bias=cbias[:, 2:3])
            sc.drain()
            sc.activation(magb[:, 1024 * u:1024 * u + 1024],
                          stage[:, 1024:2048], AF.Exp,
                          scale=0.5).then_inc(s_mag, 1)

        def ffn_dve(v, tf, u):
            ppf = 2048 * (tf % 2)
            v.wait_ge(s_mag, 2 * tf + u + 1)
            v.tensor_add(out_sb[:, ppf + 1024 * u:ppf + 1024 * u + 1024],
                         out_sb[:, ppf + 1024 * u:ppf + 1024 * u + 1024],
                         magb[:, 1024 * u:1024 * u + 1024]).then_inc(s_r2, 1)

        @blk.sync
        def _(s):
            s.dma_start(ident_sb[:, :], id_d[:, :]).then_inc(s_bld, 16)
            s.dma_start(brow[:, :], bet_d[:, :]).then_inc(s_bld, 16)
            for t in range(min(2, nch)):
                src = x_d[256 * t:256 * (t + 1), :].rearrange(
                    "(u p) d -> p u d", u=2)
                dstx = x_sb[:, 2048 * (t % 2):2048 * (t % 2) + 2048].rearrange(
                    "p (u d) -> p u d", u=2)
                s.dma_start(dstx, src).then_inc(s_x[t % 2], 16)
            for t in range(nch):
                if t >= 1:
                    s.wait_ge(s_me, t)
                    s.dma_start(memz_bf[64:128, :],
                                memz_bf[0:64, :]).then_inc(s_mcp, 16)
                if t + 2 < nch:
                    s.wait_ge(s_r1, 2 * t + 2)
                    tt = t + 2
                    src = x_d[256 * tt:256 * (tt + 1), :].rearrange(
                        "(u p) d -> p u d", u=2)
                    dstx = x_sb[:, 2048 * (tt % 2):
                                2048 * (tt % 2) + 2048].rearrange(
                        "p (u d) -> p u d", u=2)
                    s.dma_start(dstx, src).then_inc(s_x[t % 2], 16)
                if t >= 1:
                    s.wait_ge(s_r2, 2 * t)
                    tt = t - 1
                    dst = y_d[256 * tt:256 * (tt + 1), :].rearrange(
                        "(u p) d -> p u d", u=2)
                    srco = out_sb[:, 2048 * (tt % 2):
                                  2048 * (tt % 2) + 2048].rearrange(
                        "p (u d) -> p u d", u=2)
                    s.dma_start(dst, srco).then_inc(s_o[tt % 2], 16)
            s.wait_ge(s_r2, 2 * nch)
            tt = nch - 1
            dst = y_d[256 * tt:256 * (tt + 1), :].rearrange(
                "(u p) d -> p u d", u=2)
            srco = out_sb[:, 2048 * (tt % 2):2048 * (tt % 2) + 2048].rearrange(
                "p (u d) -> p u d", u=2)
            s.dma_start(dst, srco).then_inc(s_o[tt % 2], 16)
            s.wait_ge(s_o[0], 16 * ((nch + 1) // 2))
            s.wait_ge(s_o[1], 16 * (nch // 2))

        @blk.gpsimd
        def _(g):
            g.memset(memz_bf[:, :], 0.0)
            g.drain()
            g.memset(memz_bf[:, :].rearrange("p (h c) -> p h c", c=65)
                     [:, :, 64:65], 1.0 / 64.0)
            g.memset(onesr_bf[:, :], 1.0)
            g.memset(cbias[:, 0:1], PI_2)
            g.memset(cbias[:, 1:2], 1e-5)
            g.memset(cbias[:, 2:3], 1e-38)
            g.memset(onesk_bf[:, :], 1.0)
            g.drain().then_inc(s_ig, 1)
            for w_d, w_s in ((wq_d, wq_b), (wk_d, wk_b), (wv_d, wv_r),
                             (wo_d, wo_r)):
                g.dma_start(w_s[:, :].rearrange("p (k m) -> p k m",
                                                 k=w_d.shape[0] // 128),
                            w_d[:, :].rearrange("(k p) m -> p k m",
                                                p=128)).then_inc(s_wg, 16)
            srcs = []
            for a_d, n_d in ((th_d, thn_d), (ph_d, phn_d)):
                for p in range(NTH):
                    srcs.append((
                        a_d[256 * p:256 * (p + 1), :].rearrange(
                            "(gg pp) d -> pp gg d", pp=128),
                        n_d[256 * p:256 * (p + 1), :].rearrange(
                            "(gg pp) d -> pp gg d", pp=128)))
            stg3 = [stage[:, :].rearrange("p (gg d) -> p gg d", gg=2),
                    stageB[:, :].rearrange("p (gg d) -> p gg d", gg=2)]

            def emit_accum(m):
                g.wait_ge(s_tb[m % 2], 16 * (m // 2 + 1))
                g.dma_start(stg3[m % 2], srcs[m][1],
                            accum_op=ALU.add).then_inc(s_ta[m % 2], 16)

            for n in range(2 * NTH):
                if n >= 2:
                    g.wait_ge(s_ia, n + 1)
                g.dma_start(stg3[n % 2], srcs[n][0]).then_inc(s_tb[n % 2], 16)
                if n >= 1:
                    emit_accum(n - 1)
            emit_accum(2 * NTH - 1)
            for t in range(nch):
                for u in range(2):
                    if t == 0 and u == 0:
                        g.wait_ge(s_ia, 10)
                    if 2 * t + u >= 1:
                        g.wait_ge(s_mag, 2 * t + u)
                    g.wait_ge(s_sqf, 8 * t + 4 * u + 2)
                    g.tensor_mul(stage[:, 0:1024], fscr[:, 0:1024],
                                 fscr[:, 0:1024])
                    g.wait_ge(s_sqf, 8 * t + 4 * u + 4)
                    g.drain()
                    g.tensor_mul(stage[:, 1024:2048], fscr[:, 1024:2048],
                                 fscr[:, 1024:2048])
                    g.drain()
                    g.tensor_add(stage[:, 0:1024], stage[:, 0:1024],
                                 stage[:, 1024:2048]).then_inc(s_ss, 1)

        @blk.scalar
        def _(sc):
            sc.wait_ge(s_bld, 32)
            sc.activation(grow_bf[:, :], brow[:, :], AF.Sigmoid).then_inc(s_ia, 1)
            sc.drain().then_inc(s_ia, 1)

            def emit_sin(n):
                i, p = divmod(n, NTH)
                dst = cos_r if i == 0 else sin_r
                buf = stage if n % 2 == 0 else stageB
                sc.wait_ge(s_ta[n % 2], 16 * (n // 2 + 1))
                if n == 0:
                    sc.wait_ge(s_ig, 1)
                if i == 0:
                    sc.activation(dst[:, 2048 * p:2048 * (p + 1)],
                                  buf[:, :], AF.Sin,
                                  bias=cbias[:, 0:1]).then_inc(s_ia, 1)
                else:
                    sc.activation(dst[:, 2048 * p:2048 * (p + 1)],
                                  buf[:, :], AF.Sin).then_inc(s_ia, 1)

            smv3 = smv[:, :].rearrange("p (u c) -> p u c", c=2)

            def ln_chain(tt):
                sc.wait_ge(s_ln, tt + 1)
                if tt >= 1:
                    sc.wait_ge(s_xa, 2 * tt)
                sc.activation(lnv[:, :], smv3[:, :, 1:2], AF.Ln, bias=cbias[:, 1:2])
                sc.drain()
                sc.activation(rstd[:, :], lnv[:, :], AF.Exp, scale=-0.5)
                sc.drain()
                sc.activation(negr[:, :], rstd[:, :], AF.Copy, scale=-1.0)
                sc.drain()
                for u in range(2):
                    sc.activation(negmr[:, u:u + 1], smv[:, 2 * u:2 * u + 1],
                                  AF.Copy, scale=negr[:, u:u + 1])
                sc.drain().then_inc(s_lnc, 1)

            ln_chain(0)
            for n in range(2 * NTH):
                emit_sin(n)
            for t in range(nch):
                pp = 2048 * (t % 2)
                sc.wait_ge(s_mq, t + 1)
                sc.activation(scr[:, 1024:2048], scr[:, 0:1024],
                              AF.Exp).then_inc(s_eq, 1)
                sc.wait_ge(s_mk, t + 1)
                sc.activation(scr[:, 3072:4096], scr[:, 2048:3072],
                              AF.Exp).then_inc(s_ek, 1)
                for u in range(2):
                    sc.wait_ge(s_mx, 2 * t + u + 1)
                    for h in range(8):
                        jb = 2 * (h % 2) + (h // 4)
                        ib = (h // 2) % 2
                        c = 8 * u + 2 * jb + ib
                        sl = pb[2 + jb][:, 256 * ib:256 * ib + 256]
                        sc.activation(e1_bf[:, 2048 * u + 256 * h:
                                            2048 * u + 256 * h + 256],
                                      sl, AF.Exp, bias=negms[:, c:c + 1],
                                      scale=0.125, accum_out=den[:, c:c + 1])
                    sc.drain().then_inc(s_e1, 1)
                    if t >= 1:
                        ffn_evac(sc, t - 1, u)
                        if u == 1:
                            ffn_mag(sc, t - 1, 0)
                            ffn_mag(sc, t - 1, 1)
                    if u == 0 and t + 1 < nch:
                        ln_chain(t + 1)
            for u in range(2):
                ffn_evac(sc, nch - 1, u)
            ffn_mag(sc, nch - 1, 0)
            ffn_mag(sc, nch - 1, 1)

        @blk.tensor
        def _(t_):
            t_.wait_ge(s_ia, 2)
            t_.wait_ge(s_ig, 1)
            t_.wait_ge(s_bld, 32)
            t_.matmul(pb[2][:, :], onesr_bf[0:1, :], grow_bf[0:1, :],
                      start=True, stop=True).then_inc(s_ip, 1)
            for ft in range(4):
                t_.matmul(pb[3][:, ft:ft + 1],
                          grow_bf[0:1, 128 * ft:128 * (ft + 1)],
                          onesr_bf[0:1, 0:1], start=True,
                          stop=True).then_inc(s_ip, 1)
            t_.wait_ge(s_iv, 2)
            for t in range(nch):
                pp = 2048 * (t % 2)
                t_.wait_ge(s_xa, 2 * t + 2)
                if t >= 1:
                    t_.wait_ge(s_amn, 2 * t)
                    t_.wait_ge(s_bl, 2 * t)
                for dt in range(NKT):
                    if dt == 4 and t >= 1:
                        t_.wait_ge(s_r1, 2 * t)
                    for u in range(2):
                        li = t_.matmul(pb[2 + dt // 2][:, 128 * (2 * (dt % 2) + u):
                                                  128 * (2 * (dt % 2) + u) + 128],
                                  xn_bf[:, 1024 * u + 128 * dt:
                                        1024 * u + 128 * dt + 128],
                                  ident_sb[:, :], start=True, stop=True)
                li.then_inc(s_xnt, 1)
                t_.wait_ge(s_xne, t + 1)
                if t == 0:
                    t_.wait_ge(s_wg, 64)
                for mt in range(4):
                    for kt in range(NKT):
                        li = t_.matmul(pb[2 + mt // 2][:, 256 * (mt % 2):
                                                  256 * (mt % 2) + 256],
                                  wq_b[:, 512 * kt + 128 * mt:
                                       512 * kt + 128 * mt + 128],
                                  xnT_sb[:, pp + 256 * kt:pp + 256 * kt + 256],
                                  start=(kt == 0), stop=(kt == NKT - 1))
                li.then_inc(s_qt, 1)
                if t == 0:
                    t_.wait_ge(s_wg, 64)
                for mt in range(4):
                    for kt in range(NKT):
                        li = t_.matmul(pb[4 + mt // 2][:, 256 * (mt % 2):
                                                  256 * (mt % 2) + 256],
                                  wk_b[:, 512 * kt + 128 * mt:
                                       512 * kt + 128 * mt + 128],
                                  xnT_sb[:, pp + 256 * kt:pp + 256 * kt + 256],
                                  start=(kt == 0), stop=(kt == NKT - 1))
                li.then_inc(s_kt, 1)
                t_.wait_ge(s_qte, t + 1)
                if t == 0:
                    t_.wait_ge(s_wg, 64)
                for u in range(2):
                    for kt in range(NKT):
                        li = t_.matmul(pb[2 + u][:, :],
                                  xnT_sb[:, pp + 256 * kt + 128 * u:
                                         pp + 256 * kt + 128 * u + 128],
                                  wv_r[:, 512 * kt:512 * kt + 512],
                                  start=(kt == 0), stop=(kt == NKT - 1))
                    li.then_inc(s_v, 1)
                t_.wait_ge(s_kte, t + 1)
                for u in range(2):
                    for ft in range(4):
                        li = t_.matmul(pb[4 + u][:, 128 * ft:128 * ft + 128],
                                  kT_bf[:, 256 * ft + 128 * u:
                                        256 * ft + 128 * u + 128],
                                  ident_sb[:, :], start=True, stop=True)
                li.then_inc(s_knt, 1)
                for u in range(2):
                    t_.wait_ge(s_ve, 2 * t + 2)
                    t_.wait_ge(s_sk, t + 1)
                    if u == 1:
                        t_.wait_ge(s_pte, 2 * t + 1)
                    for h in (0, 2, 4, 6, 1, 3, 5, 7):
                        b = 64 * (h % 2)
                        jb = 2 * (h % 2) + (h // 4)
                        ib = (h // 2) % 2
                        li = t_.matmul(pb[2 + jb][:, 256 * ib:256 * ib + 256],
                                  qT_bf[b:b + 64, 256 * (h // 2) + 128 * u:
                                        256 * (h // 2) + 128 * u + 128],
                                  kT_bf[b:b + 64, 256 * (h // 2):
                                        256 * (h // 2) + 256],
                                  start=True, stop=True)
                    li.then_inc(s_sc, 1)
                    if t >= 1:
                        ffn_mms(t_, t - 1, u)
                    t_.wait_ge(s_pr, 2 * t + u + 1)
                    for h in range(8):
                        for kt in range(2):
                            li = t_.matmul(pb[2 + h // 2][:, 256 * (h % 2) + 128 * kt:
                                                     256 * (h % 2) + 128 * kt + 128],
                                      pr_bf[:, 2048 * u + 256 * h + 128 * kt:
                                            2048 * u + 256 * h + 128 * kt + 128],
                                      ident_sb[:, :], start=True, stop=True)
                    li.then_inc(s_pt, 1)
                for u in range(2):
                    bank_ad = pb[2] if u == 0 else pb[4]
                    t_.wait_ge(s_pte, 2 * t + 2)
                    if u == 1:
                        t_.wait_ge(s_bl, 2 * t + 1)
                        t_.wait_ge(s_amn, 2 * t + 1)
                        t_.wait_ge(s_r1, 2 * t + 1)
                    for kt in range(2):
                        for h in range(8):
                            b = 64 * (h % 2)
                            li = t_.matmul(bank_ad[b:b + 64, 128 * (h // 2):
                                              128 * (h // 2) + 128],
                                      v_bf[:, 512 * kt + 64 * h:
                                           512 * kt + 64 * h + 64],
                                      pT_sb[:, 2048 * u + 1024 * kt + 128 * h:
                                            2048 * u + 1024 * kt + 128 * h + 128],
                                      start=(kt == 0 and h <= 1),
                                      stop=(kt == 1 and h >= 6),
                                      skip_group_check=True)
                    li.then_inc(s_ad, 1)
                    if t >= 1:
                        t_.wait_ge(s_me, t)
                        t_.wait_ge(s_mcp, 16 * t)
                    else:
                        t_.wait_ge(s_ig, 1)
                    t_.wait_ge(s_sq, t + 1)
                    bkA = pb[3] if u == 0 else pb[5]
                    bkB = pb[4] if u == 0 else pb[2]
                    for h in range(8):
                        bank = bkA if h % 2 == 0 else bkB
                        col = 65 * (h // 2)
                        b = 64 * (h % 2)
                        li = t_.matmul(bank[:, col:col + 65],
                                  sqT_bf[b:b + 64, 256 * (h // 2) + 128 * u:
                                         256 * (h // 2) + 128 * u + 128],
                                  memz_bf[b:b + 64, 65 * h:65 * h + 65],
                                  start=True, stop=True)
                    li.then_inc(s_am, 1)
                    bank_mt = pb[5] if u == 0 else pb[3]
                    t_.wait_ge(s_amn, 2 * t + u + 1)
                    for ft in range(4):
                        li = t_.matmul(bank_mt[:, 128 * ft:128 * ft + 128],
                                  amn_bf[:, 128 * ft:128 * ft + 128],
                                  ident_sb[:, :], start=True, stop=True)
                    li.then_inc(s_amt, 1)
                    t_.wait_ge(s_bl, 2 * t + u + 1)
                    if t == 0 and u == 0:
                        t_.wait_ge(s_wg, 64)
                    bh0 = pb[2] if u == 0 else pb[4]
                    bh1 = pb[3] if u == 0 else pb[5]
                    for half in range(2):
                        for ft in range(4):
                            li = t_.matmul((bh0 if half == 0 else bh1)[:, :],
                                      attT_bf[:, 128 * ft:128 * ft + 128],
                                      wo_r[:, 1024 * ft + 512 * half:
                                           1024 * ft + 512 * half + 512],
                                      start=(ft == 0), stop=(ft == 3))
                    li.then_inc(s_wo, 1)
                if t >= 1:
                    t_.wait_ge(s_me, t)
                for kt in range(2):
                    for h in range(8):
                        t_.matmul(pb[0][0:64, 64 * h:64 * h + 64],
                                  sk_bf[:, 512 * kt + 64 * h:
                                        512 * kt + 64 * h + 64],
                                  v_bf[:, 512 * kt + 64 * h:
                                       512 * kt + 64 * h + 64],
                                  start=(t == 0 and kt == 0 and h == 0),
                                  stop=(t == nch - 1 and kt == 1 and h == 7),
                                  skip_group_check=True)
                        li = t_.matmul(pb[1][0:64, h:h + 1],
                                  sk_bf[:, 512 * kt + 64 * h:
                                        512 * kt + 64 * h + 64],
                                  onesk_bf[:, :],
                                  start=(t == 0 and kt == 0 and h == 0),
                                  stop=(t == nch - 1 and kt == 1 and h == 7),
                                  skip_group_check=True)
                li.then_inc(s_md, 1)
            for u in range(2):
                ffn_mms(t_, nch - 1, u)

        @blk.vector
        def _(v):
            v.wait_ge(s_ip, 1)
            v.tensor_copy(g_tile[:, :], pb[2][:, :]).then_inc(s_iv, 1)
            v.wait_ge(s_ip, 5)
            v.tensor_copy(gcol[:, :], pb[3][:, 0:4])
            v.drain()
            v.tensor_scalar(omg[:, :], gcol[:, :], -1.0, 1.0, ALU.mult, ALU.add)
            v.drain().then_inc(s_iv, 1)
            def bn_xn_stats(tt):
                pq = 2048 * (tt % 2)
                v.wait_ge(s_x[tt % 2], 16 * (tt // 2 + 1))
                if tt >= 1:
                    v.wait_ge(s_lnc, tt)
                for u in range(2):
                    v.bn_stats(stat6[:, 12 * u:12 * u + 6],
                               x_sb[:, pq + 1024 * u:pq + 1024 * u + 512])
                    v.bn_stats(stat6[:, 12 * u + 6:12 * u + 12],
                               x_sb[:, pq + 1024 * u + 512:
                                    pq + 1024 * u + 1024])
                    v.drain()
                    li = v.bn_aggr(smv[:, 2 * u:2 * u + 2],
                                   stat6[:, 12 * u:12 * u + 12])
                li.then_inc(s_ln, 1)

            def bn_xn_affine(tt):
                pq = 2048 * (tt % 2)
                v.wait_ge(s_lnc, tt + 1)
                if tt >= 1:
                    v.wait_ge(s_xnt, tt)
                for u in range(2):
                    v.tensor_scalar(xn_bf[:, 1024 * u:1024 * (u + 1)],
                                    x_sb[:, pq + 1024 * u:pq + 1024 * (u + 1)],
                                    rstd[:, u:u + 1], negmr[:, u:u + 1],
                                    ALU.mult, ALU.add).then_inc(s_xa, 1)

            bn_xn_stats(0)
            bn_xn_affine(0)
            for t in range(nch):
                pp = 2048 * (t % 2)
                v.wait_ge(s_xnt, t + 1)
                if t >= 2:
                    v.wait_ge(s_ff, 8 * (t - 2) + 8)
                for j in range(4):
                    li = v.tensor_copy(xnT_sb[:, pp + 512 * j:pp + 512 * j + 512],
                                       pb[2 + j][:, :])
                li.then_inc(s_xne, 1)
                v.wait_ge(s_qt, t + 1)
                v.tensor_copy(qT_bf[:, 0:512], pb[2][:, :])
                v.tensor_copy(qT_bf[:, 512:1024],
                              pb[3][:, :]).then_inc(s_qte, 1)
                v.drain()
                v.tensor_scalar_min(scr[:, 0:1024], qT_bf[:, :],
                                    0.0).then_inc(s_mq, 1)
                v.wait_ge(s_kt, t + 1)
                v.tensor_copy(kT_bf[:, 0:512], pb[4][:, :])
                v.tensor_copy(kT_bf[:, 512:1024],
                              pb[5][:, :]).then_inc(s_kte, 1)
                v.wait_ge(s_eq, t + 1)
                v.scalar_tensor_tensor(sqT_bf[:, :], qT_bf[:, :], 0.0,
                                       scr[:, 1024:2048], ALU.max,
                                       ALU.add).then_inc(s_sq, 1)
                for u in range(2):
                    v.wait_ge(s_v, 2 * t + u + 1)
                    v.tensor_copy(v_bf[:, 512 * u:512 * u + 512],
                                  pb[2 + u][:, :]).then_inc(s_ve, 1)
                v.wait_ge(s_knt, t + 1)
                for u in range(2):
                    li = v.tensor_scalar_min(scr[:, 2048 + 512 * u:
                                                 2048 + 512 * u + 512],
                                             pb[4 + u][:, :], 0.0)
                li.then_inc(s_mk, 1)
                v.wait_ge(s_ek, t + 1)
                for u in range(2):
                    li = v.scalar_tensor_tensor(sk_bf[:, 512 * u:512 * u + 512],
                                                pb[4 + u][:, :], 0.0,
                                                scr[:, 3072 + 512 * u:
                                                    3072 + 512 * u + 512],
                                                ALU.max, ALU.add)
                li.then_inc(s_sk, 1)
                if t >= 1:
                    v.wait_ge(s_md, t)
                    mz3 = memz_bf[0:64, :].rearrange("p (h c) -> p h c", c=65)
                    v.tensor_copy(mz3[:, :, 0:64],
                                  pb[0][0:64, :].rearrange("p (h c) -> p h c",
                                                           c=64))
                    v.tensor_scalar_add(mz3[:, :, 64:65],
                                        pb[1][0:64, 0:8].unsqueeze(2),
                                        1.0 / 64.0).then_inc(s_me, 1)
                for u in range(2):
                    v.wait_ge(s_sc, 2 * t + u + 1)
                    for j in range(4):
                        v.reduce_max(negm[:, 8 * u + 2 * j:8 * u + 2 * j + 2],
                                     pb[2 + j][:, :].rearrange(
                                         "p (i c) -> p i c", c=256),
                                     axis=AX.X, negate=True)
                    v.drain()
                    v.tensor_scalar_mul(negms[:, 8 * u:8 * u + 8],
                                        negm[:, 8 * u:8 * u + 8],
                                        0.125).then_inc(s_mx, 1)
                    if u == 0 and t + 1 < nch:
                        bn_xn_stats(t + 1)
                    v.wait_ge(s_e1, 2 * t + u + 1)
                    v.reciprocal(rdb[:, 8 * u:8 * u + 8],
                                 den[:, 8 * u:8 * u + 8])
                    v.drain()
                    for h in range(8):
                        jb = 2 * (h % 2) + (h // 4)
                        ib = (h // 2) % 2
                        c = 8 * u + 2 * jb + ib
                        li = v.tensor_scalar_mul(
                            pr_bf[:, 2048 * u + 256 * h:2048 * u + 256 * h + 256],
                            e1_bf[:, 2048 * u + 256 * h:2048 * u + 256 * h + 256],
                            rdb[:, c:c + 1])
                    li.then_inc(s_pr, 1)
                    v.wait_ge(s_pt, 2 * t + u + 1)
                    for j in range(4):
                        src = pb[2 + j][:, :].rearrange(
                            "p (i kt q) -> p i kt q", i=2, kt=2)
                        dst = pT_sb[:, 2048 * u:2048 * u + 2048].rearrange(
                            "p (kt h q) -> p h kt q", kt=2, h=8)[:, 2 * j:2 * j + 2]
                        li = v.tensor_copy(dst, src)
                    li.then_inc(s_pte, 1)
                for u in range(2):
                    bkA = pb[3] if u == 0 else pb[5]
                    bkB = pb[4] if u == 0 else pb[2]
                    bank_mt = pb[5] if u == 0 else pb[3]
                    bank_ad = pb[2] if u == 0 else pb[4]
                    v.wait_ge(s_am, 2 * t + u + 1)
                    a3 = bkA[:, 0:260].rearrange("p (j c) -> p j c", c=65)
                    b3 = bkB[:, 0:260].rearrange("p (j c) -> p j c", c=65)
                    rm3 = rmb[:, 0:8].rearrange("p (j i) -> p j i", i=2)
                    v.reciprocal(rm3[:, :, 0:1], a3[:, :, 64:65])
                    v.reciprocal(rm3[:, :, 1:2], b3[:, :, 64:65])
                    v.drain()
                    v.tensor_tensor(m2t[:, :],
                                    rmb[:, 0:8].unsqueeze(2).broadcast_to(
                                        [128, 8, 64]),
                                    g_tile[:, :].rearrange("p (h c) -> p h c",
                                                           c=64), ALU.mult)
                    v.drain()
                    amn4 = amn_bf[:, :].rearrange("p (j i c) -> p j i c",
                                                  i=2, c=64)
                    m24 = m2t[:, :].rearrange("p (j i c) -> p j i c",
                                              i=2, c=64)
                    v.tensor_tensor(amn4[:, :, 0:1, :],
                                    a3[:, :, 0:64].unsqueeze(2),
                                    m24[:, :, 0:1, :], ALU.mult)
                    v.tensor_tensor(amn4[:, :, 1:2, :],
                                    b3[:, :, 0:64].unsqueeze(2),
                                    m24[:, :, 1:2, :],
                                    ALU.mult).then_inc(s_amn, 1)
                    v.wait_ge(s_amt, 2 * t + u + 1)
                    v.tensor_copy(amT_sb[:, :], bank_mt[:, :])
                    v.drain()
                    v.wait_ge(s_ad, 2 * t + u + 1)
                    for ft in range(4):
                        li = v.scalar_tensor_tensor(
                            attT_bf[:, 128 * ft:128 * ft + 128],
                            bank_ad[:, 128 * ft:128 * ft + 128],
                            omg[:, ft:ft + 1],
                            amT_sb[:, 128 * ft:128 * ft + 128],
                            ALU.mult, ALU.add)
                    li.then_inc(s_bl, 1)
                    if u == 0 and t + 1 < nch:
                        bn_xn_affine(t + 1)
                    v.wait_ge(s_wo, 2 * t + u + 1)
                    if t >= 2 and u == 0:
                        v.wait_ge(s_o[t % 2], 16 * (t // 2))
                    bh0 = pb[2] if u == 0 else pb[4]
                    bh1 = pb[3] if u == 0 else pb[5]
                    v.tensor_add(out_sb[:, pp + 1024 * u:pp + 1024 * u + 512],
                                 x_sb[:, pp + 1024 * u:pp + 1024 * u + 512],
                                 bh0[:, :])
                    v.tensor_add(out_sb[:, pp + 1024 * u + 512:
                                        pp + 1024 * u + 1024],
                                 x_sb[:, pp + 1024 * u + 512:
                                      pp + 1024 * u + 1024],
                                 bh1[:, :]).then_inc(s_r1, 1)
                if u == 1 and t >= 1:
                    ffn_dve(v, t - 1, 0)
                    ffn_dve(v, t - 1, 1)
            for u in range(2):
                ffn_dve(v, nch - 1, u)

    return nc


def kernel(L0_attn_x, L0_attn_Wq, L0_attn_Wk, L0_attn_Wv, L0_attn_betas,
           L0_attn_Wo, L0_ffn_theta, L0_ffn_phi, L0_ffn_theta_noise,
           L0_ffn_phi_noise):
    x = np.asarray(L0_attn_x)
    nc = build_nc()
    common = {
        "wq": np.ascontiguousarray(L0_attn_Wq),
        "wk": np.ascontiguousarray(L0_attn_Wk),
        "wv": np.ascontiguousarray(L0_attn_Wv),
        "wo": np.ascontiguousarray(L0_attn_Wo),
        "betas": np.ascontiguousarray(np.asarray(L0_attn_betas).reshape(1, 512)),
        "th": np.ascontiguousarray(L0_ffn_theta),
        "thn": np.ascontiguousarray(L0_ffn_theta_noise),
        "ph": np.ascontiguousarray(L0_ffn_phi),
        "phn": np.ascontiguousarray(L0_ffn_phi_noise),
    }
    in_maps = [dict(common, x=np.ascontiguousarray(x[b])) for b in range(NCORE)]
    res = run_bass_kernel_spmd(nc, in_maps, core_ids=list(range(NCORE)))
    out = np.stack([np.asarray(res.results[b]["y"]) for b in range(NCORE)],
                   axis=0)
    return out


# version 17
# speedup vs baseline: 1.0455x; 1.0049x over previous
import contextlib
import numpy as np
import ml_dtypes
import concourse.bass as bass
import concourse.mybir as mybir
from concourse.bass_utils import run_bass_kernel_spmd

F32 = mybir.dt.float32
BF16 = mybir.dt.bfloat16
AF = mybir.ActivationFunctionType
ALU = mybir.AluOpType
AX = mybir.AxisListType

NCORE = 8
S, D = 4096, 1024
NKT = D // 128
PI_2 = float(np.pi / 2)
NTH = 4


def build_nc(nch=16):
    nc = bass.Bass("TRN2", target_bir_lowering=False)
    x_d = nc.dram_tensor("x", [S, D], F32, kind="ExternalInput").ap()
    wq_d = nc.dram_tensor("wq", [D, 512], F32, kind="ExternalInput").ap()
    wk_d = nc.dram_tensor("wk", [D, 512], F32, kind="ExternalInput").ap()
    wv_d = nc.dram_tensor("wv", [D, 512], F32, kind="ExternalInput").ap()
    wo_d = nc.dram_tensor("wo", [512, D], F32, kind="ExternalInput").ap()
    bet_d = nc.dram_tensor("betas", [1, 512], F32, kind="ExternalInput").ap()
    th_d = nc.dram_tensor("th", [D, D], F32, kind="ExternalInput").ap()
    thn_d = nc.dram_tensor("thn", [D, D], F32, kind="ExternalInput").ap()
    ph_d = nc.dram_tensor("ph", [D, D], F32, kind="ExternalInput").ap()
    phn_d = nc.dram_tensor("phn", [D, D], F32, kind="ExternalInput").ap()
    y_d = nc.dram_tensor("y", [S, D], F32, kind="ExternalOutput").ap()
    id_d = nc.inline_tensor(np.eye(128, dtype=ml_dtypes.bfloat16), "identd").ap()

    es = contextlib.ExitStack()
    sbt = lambda name, shape, dt: es.enter_context(nc.sbuf_tensor(name, shape, dt))
    pst = lambda name, shape: es.enter_context(nc.psum_tensor(name, shape, F32))
    sem = lambda name: es.enter_context(nc.semaphore(name))

    wq_b = sbt("wq_b", [128, 4096], BF16)
    wk_b = sbt("wk_b", [128, 4096], BF16)
    wv_r = sbt("wv_r", [128, 4096], BF16)
    wo_r = sbt("wo_r", [128, 4096], BF16)
    cos_r = sbt("cos_r", [128, 8192], BF16)
    sin_r = sbt("sin_r", [128, 8192], BF16)
    stage = sbt("stage", [128, 2048], F32)
    stageB = sbt("stageB", [128, 2048], F32)
    x_sb = sbt("x_sb", [128, 4096], F32)
    out_sb = sbt("out_sb", [128, 4096], F32)
    xn_bf = sbt("xn_bf", [128, 2048], BF16)
    xnT_sb = sbt("xnT_sb", [128, 4096], BF16)
    qT_bf = sbt("qT_bf", [128, 1024], BF16)
    sqT_bf = sbt("sqT_bf", [128, 1024], BF16)
    kT_bf = sbt("kT_bf", [128, 1024], BF16)
    sk_bf = sbt("sk_bf", [128, 1024], BF16)
    v_bf = sbt("v_bf", [128, 1024], BF16)
    e1_bf = sbt("e1_bf", [128, 4096], BF16)
    pr_bf = sbt("pr_bf", [128, 4096], BF16)
    pT_sb = sbt("pT_sb", [128, 4096], BF16)
    amn_bf = sbt("amn_bf", [128, 512], BF16)
    amT_sb = sbt("amT_sb", [128, 512], BF16)
    attT_bf = sbt("attT_bf", [128, 512], BF16)
    scr = sbt("scr", [128, 4096], F32)
    memz_bf = sbt("memz_bf", [128, 520], BF16)
    g_tile = sbt("g_tile", [128, 512], BF16)
    gcol = sbt("gcol", [128, 4], F32)
    omg = sbt("omg", [128, 4], F32)
    brow = sbt("brow", [1, 512], F32)
    grow_bf = sbt("grow_bf", [1, 512], BF16)
    onesr_bf = sbt("onesr_bf", [1, 128], BF16)
    onesk_bf = sbt("onesk_bf", [128, 1], BF16)
    ident_sb = sbt("ident_sb", [128, 128], BF16)
    stat6 = sbt("stat6", [128, 24], F32)
    smv = sbt("smv", [128, 4], F32)
    lnv = sbt("lnv", [128, 2], F32)
    rstd = sbt("rstd", [128, 2], F32)
    negr = sbt("negr", [128, 2], F32)
    negmr = sbt("negmr", [128, 2], F32)
    negm = sbt("negm", [128, 16], F32)
    negms = sbt("negms", [128, 16], F32)
    den = sbt("den", [128, 16], F32)
    rdb = sbt("rdb", [128, 16], F32)
    rmb = sbt("rmb", [128, 8], F32)
    m2t = sbt("m2t", [128, 512], F32)
    cbias = sbt("cbias", [128, 3], F32)
    fscr = sbt("fscr", [128, 2048], BF16)
    magb = sbt("magb", [128, 2048], BF16)

    pb = [pst(f"pb{i}", [128, 512]) for i in range(8)]

    s_x = [sem("s_x0"), sem("s_x1")]; s_o = [sem("s_o0"), sem("s_o1")]; s_mcp = sem("s_mcp")
    s_wg = sem("s_wg"); s_bld = sem("s_bld")
    s_tb = [sem("s_tb0"), sem("s_tb1")]; s_ta = [sem("s_ta0"), sem("s_ta1")]
    s_ia = sem("s_ia"); s_iv = sem("s_iv"); s_ip = sem("s_ip"); s_ig = sem("s_ig")
    s_ln = sem("s_ln"); s_xa = sem("s_xa"); s_lnc = sem("s_lnc")
    s_xnt = sem("s_xnt"); s_xne = sem("s_xne")
    s_qt = sem("s_qt"); s_qte = sem("s_qte")
    s_mq = sem("s_mq"); s_eq = sem("s_eq"); s_sq = sem("s_sq")
    s_kt = sem("s_kt"); s_kte = sem("s_kte"); s_knt = sem("s_knt")
    s_mk = sem("s_mk"); s_ek = sem("s_ek"); s_sk = sem("s_sk")
    s_v = sem("s_v"); s_ve = sem("s_ve")
    s_sc = sem("s_sc"); s_mx = sem("s_mx"); s_e1 = sem("s_e1")
    s_pr = sem("s_pr"); s_pt = sem("s_pt"); s_pte = sem("s_pte")
    s_me = sem("s_me"); s_am = sem("s_am"); s_amn = sem("s_amn")
    s_amt = sem("s_amt"); s_ad = sem("s_ad"); s_bl = sem("s_bl")
    s_wo = sem("s_wo"); s_r1 = sem("s_r1")
    s_ff = sem("s_ff"); s_sqf = sem("s_sqf"); s_ss = sem("s_ss")
    s_mag = sem("s_mag"); s_r2 = sem("s_r2"); s_md = sem("s_md")

    blk = es.enter_context(nc.Block())
    with es:
        def ffn_mms(t_, tf, u):
            ppf = 2048 * (tf % 2)
            for mi, w_s in enumerate((cos_r, sin_r)):
                if tf == 0 and u == 0:
                    t_.wait_ge(s_ia, 6 if mi == 0 else 10)
                for half in range(2):
                    n = 8 * tf + 4 * u + 2 * mi + half
                    if n >= 2:
                        t_.wait_ge(s_sqf, n - 1)
                    for dt in range(NKT):
                        li = t_.matmul(pb[6 + half][:, :],
                                  xnT_sb[:, ppf + 256 * dt + 128 * u:
                                         ppf + 256 * dt + 128 * u + 128],
                                  w_s[:, 1024 * dt + 512 * half:
                                      1024 * dt + 512 * half + 512],
                                  start=(dt == 0), stop=(dt == NKT - 1))
                    li.then_inc(s_ff, 1)

        def ffn_evac(sc, tf, u):
            if tf >= 1 or u >= 1:
                sc.wait_ge(s_ss, 2 * tf + u)
            sc.drain()
            for mi in range(2):
                for half in range(2):
                    sc.wait_ge(s_ff, 8 * tf + 4 * u + 2 * mi + half + 1)
                    sc.activation(fscr[:, 1024 * mi + 512 * half:
                                       1024 * mi + 512 * half + 512],
                                  pb[6 + half][:, :],
                                  AF.Copy).then_inc(s_sqf, 1)

        def ffn_mag(sc, tf, u):
            if tf >= 1:
                sc.wait_ge(s_r2, 2 * (tf - 1) + u + 1)
            sc.wait_ge(s_ss, 2 * tf + u + 1)
            sc.drain()
            sc.activation(stage[:, 1024:2048], stage[:, 0:1024], AF.Ln,
                          bias=cbias[:, 2:3])
            sc.drain()
            sc.activation(magb[:, 1024 * u:1024 * u + 1024],
                          stage[:, 1024:2048], AF.Exp,
                          scale=0.5).then_inc(s_mag, 1)

        def ffn_dve(v, tf, u):
            ppf = 2048 * (tf % 2)
            v.wait_ge(s_mag, 2 * tf + u + 1)
            v.tensor_add(out_sb[:, ppf + 1024 * u:ppf + 1024 * u + 1024],
                         out_sb[:, ppf + 1024 * u:ppf + 1024 * u + 1024],
                         magb[:, 1024 * u:1024 * u + 1024]).then_inc(s_r2, 1)

        @blk.sync
        def _(s):
            s.dma_start(ident_sb[:, :], id_d[:, :]).then_inc(s_bld, 16)
            s.dma_start(brow[:, :], bet_d[:, :]).then_inc(s_bld, 16)
            for t in range(min(2, nch)):
                src = x_d[256 * t:256 * (t + 1), :].rearrange(
                    "(u p) d -> p u d", u=2)
                dstx = x_sb[:, 2048 * (t % 2):2048 * (t % 2) + 2048].rearrange(
                    "p (u d) -> p u d", u=2)
                s.dma_start(dstx, src).then_inc(s_x[t % 2], 16)
            for t in range(nch):
                if t >= 1:
                    s.wait_ge(s_me, t)
                    s.dma_start(memz_bf[64:128, :],
                                memz_bf[0:64, :]).then_inc(s_mcp, 16)
                if t + 2 < nch:
                    s.wait_ge(s_r1, 2 * t + 2)
                    tt = t + 2
                    src = x_d[256 * tt:256 * (tt + 1), :].rearrange(
                        "(u p) d -> p u d", u=2)
                    dstx = x_sb[:, 2048 * (tt % 2):
                                2048 * (tt % 2) + 2048].rearrange(
                        "p (u d) -> p u d", u=2)
                    s.dma_start(dstx, src).then_inc(s_x[t % 2], 16)
                if t >= 1:
                    s.wait_ge(s_r2, 2 * t)
                    tt = t - 1
                    dst = y_d[256 * tt:256 * (tt + 1), :].rearrange(
                        "(u p) d -> p u d", u=2)
                    srco = out_sb[:, 2048 * (tt % 2):
                                  2048 * (tt % 2) + 2048].rearrange(
                        "p (u d) -> p u d", u=2)
                    s.dma_start(dst, srco).then_inc(s_o[tt % 2], 16)
            s.wait_ge(s_r2, 2 * nch)
            tt = nch - 1
            dst = y_d[256 * tt:256 * (tt + 1), :].rearrange(
                "(u p) d -> p u d", u=2)
            srco = out_sb[:, 2048 * (tt % 2):2048 * (tt % 2) + 2048].rearrange(
                "p (u d) -> p u d", u=2)
            s.dma_start(dst, srco).then_inc(s_o[tt % 2], 16)
            s.wait_ge(s_o[0], 16 * ((nch + 1) // 2))
            s.wait_ge(s_o[1], 16 * (nch // 2))

        @blk.gpsimd
        def _(g):
            g.memset(memz_bf[:, :], 0.0)
            g.drain()
            g.memset(memz_bf[:, :].rearrange("p (h c) -> p h c", c=65)
                     [:, :, 64:65], 1.0 / 64.0)
            g.memset(onesr_bf[:, :], 1.0)
            g.memset(cbias[:, 0:1], PI_2)
            g.memset(cbias[:, 1:2], 1e-5)
            g.memset(cbias[:, 2:3], 1e-38)
            g.memset(onesk_bf[:, :], 1.0)
            g.drain().then_inc(s_ig, 1)
            for w_d, w_s in ((wq_d, wq_b), (wk_d, wk_b), (wv_d, wv_r),
                             (wo_d, wo_r)):
                g.dma_start(w_s[:, :].rearrange("p (k m) -> p k m",
                                                 k=w_d.shape[0] // 128),
                            w_d[:, :].rearrange("(k p) m -> p k m",
                                                p=128)).then_inc(s_wg, 16)
            srcs = []
            for a_d, n_d in ((th_d, thn_d), (ph_d, phn_d)):
                for p in range(NTH):
                    srcs.append((
                        a_d[256 * p:256 * (p + 1), :].rearrange(
                            "(gg pp) d -> pp gg d", pp=128),
                        n_d[256 * p:256 * (p + 1), :].rearrange(
                            "(gg pp) d -> pp gg d", pp=128)))
            stg3 = [stage[:, :].rearrange("p (gg d) -> p gg d", gg=2),
                    stageB[:, :].rearrange("p (gg d) -> p gg d", gg=2)]

            def emit_accum(m):
                g.wait_ge(s_tb[m % 2], 16 * (m // 2 + 1))
                g.dma_start(stg3[m % 2], srcs[m][1],
                            accum_op=ALU.add).then_inc(s_ta[m % 2], 16)

            for n in range(2 * NTH):
                if n >= 2:
                    g.wait_ge(s_ia, n + 1)
                g.dma_start(stg3[n % 2], srcs[n][0]).then_inc(s_tb[n % 2], 16)
                if n >= 1:
                    emit_accum(n - 1)
            emit_accum(2 * NTH - 1)
            for t in range(nch):
                for u in range(2):
                    if t == 0 and u == 0:
                        g.wait_ge(s_ia, 10)
                    if 2 * t + u >= 1:
                        g.wait_ge(s_mag, 2 * t + u)
                    g.wait_ge(s_sqf, 8 * t + 4 * u + 2)
                    g.tensor_mul(stage[:, 0:1024], fscr[:, 0:1024],
                                 fscr[:, 0:1024])
                    g.wait_ge(s_sqf, 8 * t + 4 * u + 4)
                    g.drain()
                    g.tensor_mul(stage[:, 1024:2048], fscr[:, 1024:2048],
                                 fscr[:, 1024:2048])
                    g.drain()
                    g.tensor_add(stage[:, 0:1024], stage[:, 0:1024],
                                 stage[:, 1024:2048]).then_inc(s_ss, 1)

        @blk.scalar
        def _(sc):
            sc.wait_ge(s_bld, 32)
            sc.activation(grow_bf[:, :], brow[:, :], AF.Sigmoid).then_inc(s_ia, 1)
            sc.drain().then_inc(s_ia, 1)

            def emit_sin(n):
                i, p = divmod(n, NTH)
                dst = cos_r if i == 0 else sin_r
                buf = stage if n % 2 == 0 else stageB
                sc.wait_ge(s_ta[n % 2], 16 * (n // 2 + 1))
                if n == 0:
                    sc.wait_ge(s_ig, 1)
                if i == 0:
                    sc.activation(dst[:, 2048 * p:2048 * (p + 1)],
                                  buf[:, :], AF.Sin,
                                  bias=cbias[:, 0:1]).then_inc(s_ia, 1)
                else:
                    sc.activation(dst[:, 2048 * p:2048 * (p + 1)],
                                  buf[:, :], AF.Sin).then_inc(s_ia, 1)

            smv3 = smv[:, :].rearrange("p (u c) -> p u c", c=2)

            def ln_chain(tt):
                sc.wait_ge(s_ln, tt + 1)
                if tt >= 1:
                    sc.wait_ge(s_xa, 2 * tt)
                sc.activation(lnv[:, :], smv3[:, :, 1:2], AF.Ln, bias=cbias[:, 1:2])
                sc.drain()
                sc.activation(rstd[:, :], lnv[:, :], AF.Exp, scale=-0.5)
                sc.drain()
                sc.activation(negr[:, :], rstd[:, :], AF.Copy, scale=-1.0)
                sc.drain()
                for u in range(2):
                    sc.activation(negmr[:, u:u + 1], smv[:, 2 * u:2 * u + 1],
                                  AF.Copy, scale=negr[:, u:u + 1])
                sc.drain().then_inc(s_lnc, 1)

            ln_chain(0)
            for n in range(2 * NTH):
                emit_sin(n)
            for t in range(nch):
                pp = 2048 * (t % 2)
                sc.wait_ge(s_mq, t + 1)
                sc.activation(scr[:, 1024:2048], scr[:, 0:1024],
                              AF.Exp).then_inc(s_eq, 1)
                sc.wait_ge(s_mk, t + 1)
                sc.activation(scr[:, 3072:4096], scr[:, 2048:3072],
                              AF.Exp).then_inc(s_ek, 1)
                for u in range(2):
                    sc.wait_ge(s_mx, 2 * t + u + 1)
                    for h in range(8):
                        jb = 2 * (h % 2) + (h // 4)
                        ib = (h // 2) % 2
                        c = 8 * u + 2 * jb + ib
                        sl = pb[2 + jb][:, 256 * ib:256 * ib + 256]
                        sc.activation(e1_bf[:, 2048 * u + 256 * h:
                                            2048 * u + 256 * h + 256],
                                      sl, AF.Exp, bias=negms[:, c:c + 1],
                                      scale=0.125, accum_out=den[:, c:c + 1])
                    sc.drain().then_inc(s_e1, 1)
                    if t >= 1:
                        ffn_evac(sc, t - 1, u)
                        if u == 1:
                            ffn_mag(sc, t - 1, 0)
                            ffn_mag(sc, t - 1, 1)
                    if u == 0 and t + 1 < nch:
                        ln_chain(t + 1)
            for u in range(2):
                ffn_evac(sc, nch - 1, u)
            ffn_mag(sc, nch - 1, 0)
            ffn_mag(sc, nch - 1, 1)

        @blk.tensor
        def _(t_):
            t_.wait_ge(s_ia, 2)
            t_.wait_ge(s_ig, 1)
            t_.wait_ge(s_bld, 32)
            t_.matmul(pb[2][:, :], onesr_bf[0:1, :], grow_bf[0:1, :],
                      start=True, stop=True).then_inc(s_ip, 1)
            for ft in range(4):
                t_.matmul(pb[3][:, ft:ft + 1],
                          grow_bf[0:1, 128 * ft:128 * (ft + 1)],
                          onesr_bf[0:1, 0:1], start=True,
                          stop=True).then_inc(s_ip, 1)
            t_.wait_ge(s_iv, 2)
            for t in range(nch):
                pp = 2048 * (t % 2)
                t_.wait_ge(s_xa, 2 * t + 2)
                if t >= 1:
                    t_.wait_ge(s_amn, 2 * t)
                    t_.wait_ge(s_bl, 2 * t)
                for dt in range(NKT):
                    if dt == 4 and t >= 1:
                        t_.wait_ge(s_r1, 2 * t)
                    for u in range(2):
                        li = t_.matmul(pb[2 + dt // 2][:, 128 * (2 * (dt % 2) + u):
                                                  128 * (2 * (dt % 2) + u) + 128],
                                  xn_bf[:, 1024 * u + 128 * dt:
                                        1024 * u + 128 * dt + 128],
                                  ident_sb[:, :], start=True, stop=True)
                li.then_inc(s_xnt, 1)
                t_.wait_ge(s_xne, t + 1)
                if t == 0:
                    t_.wait_ge(s_wg, 64)
                for mt in range(4):
                    for kt in range(NKT):
                        li = t_.matmul(pb[2 + mt // 2][:, 256 * (mt % 2):
                                                  256 * (mt % 2) + 256],
                                  wq_b[:, 512 * kt + 128 * mt:
                                       512 * kt + 128 * mt + 128],
                                  xnT_sb[:, pp + 256 * kt:pp + 256 * kt + 256],
                                  start=(kt == 0), stop=(kt == NKT - 1))
                li.then_inc(s_qt, 1)
                if t == 0:
                    t_.wait_ge(s_wg, 64)
                for mt in range(4):
                    for kt in range(NKT):
                        li = t_.matmul(pb[4 + mt // 2][:, 256 * (mt % 2):
                                                  256 * (mt % 2) + 256],
                                  wk_b[:, 512 * kt + 128 * mt:
                                       512 * kt + 128 * mt + 128],
                                  xnT_sb[:, pp + 256 * kt:pp + 256 * kt + 256],
                                  start=(kt == 0), stop=(kt == NKT - 1))
                li.then_inc(s_kt, 1)
                t_.wait_ge(s_qte, t + 1)
                if t == 0:
                    t_.wait_ge(s_wg, 64)
                for u in range(2):
                    for kt in range(NKT):
                        li = t_.matmul(pb[2 + u][:, :],
                                  xnT_sb[:, pp + 256 * kt + 128 * u:
                                         pp + 256 * kt + 128 * u + 128],
                                  wv_r[:, 512 * kt:512 * kt + 512],
                                  start=(kt == 0), stop=(kt == NKT - 1))
                    li.then_inc(s_v, 1)
                t_.wait_ge(s_kte, t + 1)
                if t >= 2:
                    t_.wait_ge(s_sqf, 8 * (t - 1))
                for u in range(2):
                    for ft in range(4):
                        li = t_.matmul(pb[6 + u][:, 128 * ft:128 * ft + 128],
                                  kT_bf[:, 256 * ft + 128 * u:
                                        256 * ft + 128 * u + 128],
                                  ident_sb[:, :], start=True, stop=True)
                li.then_inc(s_knt, 1)
                for u in range(2):
                    t_.wait_ge(s_ve, 2 * t + 2)
                    if u == 1:
                        t_.wait_ge(s_pte, 2 * t + 1)
                    for h in (0, 2, 4, 6, 1, 3, 5, 7):
                        b = 64 * (h % 2)
                        jb = 2 * (h % 2) + (h // 4)
                        ib = (h // 2) % 2
                        li = t_.matmul(pb[2 + jb][:, 256 * ib:256 * ib + 256],
                                  qT_bf[b:b + 64, 256 * (h // 2) + 128 * u:
                                        256 * (h // 2) + 128 * u + 128],
                                  kT_bf[b:b + 64, 256 * (h // 2):
                                        256 * (h // 2) + 256],
                                  start=True, stop=True)
                    li.then_inc(s_sc, 1)
                    if t >= 1:
                        if u == 0:
                            t_.wait_ge(s_sk, t + 1)
                        ffn_mms(t_, t - 1, u)
                    t_.wait_ge(s_pr, 2 * t + u + 1)
                    for h in range(8):
                        for kt in range(2):
                            li = t_.matmul(pb[2 + h // 2][:, 256 * (h % 2) + 128 * kt:
                                                     256 * (h % 2) + 128 * kt + 128],
                                      pr_bf[:, 2048 * u + 256 * h + 128 * kt:
                                            2048 * u + 256 * h + 128 * kt + 128],
                                      ident_sb[:, :], start=True, stop=True)
                    li.then_inc(s_pt, 1)
                for u in range(2):
                    bank_ad = pb[2] if u == 0 else pb[4]
                    t_.wait_ge(s_pte, 2 * t + 2)
                    if u == 1:
                        t_.wait_ge(s_bl, 2 * t + 1)
                        t_.wait_ge(s_amn, 2 * t + 1)
                        t_.wait_ge(s_r1, 2 * t + 1)
                    for kt in range(2):
                        for h in range(8):
                            b = 64 * (h % 2)
                            li = t_.matmul(bank_ad[b:b + 64, 128 * (h // 2):
                                              128 * (h // 2) + 128],
                                      v_bf[:, 512 * kt + 64 * h:
                                           512 * kt + 64 * h + 64],
                                      pT_sb[:, 2048 * u + 1024 * kt + 128 * h:
                                            2048 * u + 1024 * kt + 128 * h + 128],
                                      start=(kt == 0 and h <= 1),
                                      stop=(kt == 1 and h >= 6),
                                      skip_group_check=True)
                    li.then_inc(s_ad, 1)
                    if t >= 1:
                        t_.wait_ge(s_me, t)
                        t_.wait_ge(s_mcp, 16 * t)
                    else:
                        t_.wait_ge(s_ig, 1)
                    t_.wait_ge(s_sq, t + 1)
                    bkA = pb[3] if u == 0 else pb[5]
                    bkB = pb[4] if u == 0 else pb[2]
                    for h in range(8):
                        bank = bkA if h % 2 == 0 else bkB
                        col = 65 * (h // 2)
                        b = 64 * (h % 2)
                        li = t_.matmul(bank[:, col:col + 65],
                                  sqT_bf[b:b + 64, 256 * (h // 2) + 128 * u:
                                         256 * (h // 2) + 128 * u + 128],
                                  memz_bf[b:b + 64, 65 * h:65 * h + 65],
                                  start=True, stop=True)
                    li.then_inc(s_am, 1)
                    bank_mt = pb[5] if u == 0 else pb[3]
                    t_.wait_ge(s_amn, 2 * t + u + 1)
                    for ft in range(4):
                        li = t_.matmul(bank_mt[:, 128 * ft:128 * ft + 128],
                                  amn_bf[:, 128 * ft:128 * ft + 128],
                                  ident_sb[:, :], start=True, stop=True)
                    li.then_inc(s_amt, 1)
                    t_.wait_ge(s_bl, 2 * t + u + 1)
                    if t == 0 and u == 0:
                        t_.wait_ge(s_wg, 64)
                    bh0 = pb[2] if u == 0 else pb[4]
                    bh1 = pb[3] if u == 0 else pb[5]
                    for half in range(2):
                        for ft in range(4):
                            li = t_.matmul((bh0 if half == 0 else bh1)[:, :],
                                      attT_bf[:, 128 * ft:128 * ft + 128],
                                      wo_r[:, 1024 * ft + 512 * half:
                                           1024 * ft + 512 * half + 512],
                                      start=(ft == 0), stop=(ft == 3))
                    li.then_inc(s_wo, 1)
                if t >= 1:
                    t_.wait_ge(s_me, t)
                for kt in range(2):
                    for h in range(8):
                        t_.matmul(pb[0][0:64, 64 * h:64 * h + 64],
                                  sk_bf[:, 512 * kt + 64 * h:
                                        512 * kt + 64 * h + 64],
                                  v_bf[:, 512 * kt + 64 * h:
                                       512 * kt + 64 * h + 64],
                                  start=(t == 0 and kt == 0 and h == 0),
                                  stop=(t == nch - 1 and kt == 1 and h == 7),
                                  skip_group_check=True)
                        li = t_.matmul(pb[1][0:64, h:h + 1],
                                  sk_bf[:, 512 * kt + 64 * h:
                                        512 * kt + 64 * h + 64],
                                  onesk_bf[:, :],
                                  start=(t == 0 and kt == 0 and h == 0),
                                  stop=(t == nch - 1 and kt == 1 and h == 7),
                                  skip_group_check=True)
                li.then_inc(s_md, 1)
            t_.wait_ge(s_sk, nch)
            for u in range(2):
                ffn_mms(t_, nch - 1, u)

        @blk.vector
        def _(v):
            v.wait_ge(s_ip, 1)
            v.tensor_copy(g_tile[:, :], pb[2][:, :]).then_inc(s_iv, 1)
            v.wait_ge(s_ip, 5)
            v.tensor_copy(gcol[:, :], pb[3][:, 0:4])
            v.drain()
            v.tensor_scalar(omg[:, :], gcol[:, :], -1.0, 1.0, ALU.mult, ALU.add)
            v.drain().then_inc(s_iv, 1)
            def bn_xn_stats(tt):
                pq = 2048 * (tt % 2)
                v.wait_ge(s_x[tt % 2], 16 * (tt // 2 + 1))
                if tt >= 1:
                    v.wait_ge(s_lnc, tt)
                for u in range(2):
                    v.bn_stats(stat6[:, 12 * u:12 * u + 6],
                               x_sb[:, pq + 1024 * u:pq + 1024 * u + 512])
                    v.bn_stats(stat6[:, 12 * u + 6:12 * u + 12],
                               x_sb[:, pq + 1024 * u + 512:
                                    pq + 1024 * u + 1024])
                    v.drain()
                    li = v.bn_aggr(smv[:, 2 * u:2 * u + 2],
                                   stat6[:, 12 * u:12 * u + 12])
                li.then_inc(s_ln, 1)

            def bn_xn_affine(tt):
                pq = 2048 * (tt % 2)
                v.wait_ge(s_lnc, tt + 1)
                if tt >= 1:
                    v.wait_ge(s_xnt, tt)
                for u in range(2):
                    v.tensor_scalar(xn_bf[:, 1024 * u:1024 * (u + 1)],
                                    x_sb[:, pq + 1024 * u:pq + 1024 * (u + 1)],
                                    rstd[:, u:u + 1], negmr[:, u:u + 1],
                                    ALU.mult, ALU.add).then_inc(s_xa, 1)

            bn_xn_stats(0)
            bn_xn_affine(0)
            for t in range(nch):
                pp = 2048 * (t % 2)
                v.wait_ge(s_xnt, t + 1)
                if t >= 2:
                    v.wait_ge(s_ff, 8 * (t - 2) + 8)
                for j in range(4):
                    li = v.tensor_copy(xnT_sb[:, pp + 512 * j:pp + 512 * j + 512],
                                       pb[2 + j][:, :])
                li.then_inc(s_xne, 1)
                v.wait_ge(s_qt, t + 1)
                v.tensor_copy(qT_bf[:, 0:512], pb[2][:, :])
                v.tensor_copy(qT_bf[:, 512:1024],
                              pb[3][:, :]).then_inc(s_qte, 1)
                v.drain()
                v.tensor_scalar_min(scr[:, 0:1024], qT_bf[:, :],
                                    0.0).then_inc(s_mq, 1)
                v.wait_ge(s_kt, t + 1)
                v.tensor_copy(kT_bf[:, 0:512], pb[4][:, :])
                v.tensor_copy(kT_bf[:, 512:1024],
                              pb[5][:, :]).then_inc(s_kte, 1)
                v.wait_ge(s_eq, t + 1)
                v.scalar_tensor_tensor(sqT_bf[:, :], qT_bf[:, :], 0.0,
                                       scr[:, 1024:2048], ALU.max,
                                       ALU.add).then_inc(s_sq, 1)
                for u in range(2):
                    v.wait_ge(s_v, 2 * t + u + 1)
                    v.tensor_copy(v_bf[:, 512 * u:512 * u + 512],
                                  pb[2 + u][:, :]).then_inc(s_ve, 1)
                v.wait_ge(s_knt, t + 1)
                for u in range(2):
                    li = v.tensor_scalar_min(scr[:, 2048 + 512 * u:
                                                 2048 + 512 * u + 512],
                                             pb[6 + u][:, :], 0.0)
                li.then_inc(s_mk, 1)
                v.wait_ge(s_ek, t + 1)
                for u in range(2):
                    li = v.scalar_tensor_tensor(sk_bf[:, 512 * u:512 * u + 512],
                                                pb[6 + u][:, :], 0.0,
                                                scr[:, 3072 + 512 * u:
                                                    3072 + 512 * u + 512],
                                                ALU.max, ALU.add)
                li.then_inc(s_sk, 1)
                if t >= 1:
                    v.wait_ge(s_md, t)
                    mz3 = memz_bf[0:64, :].rearrange("p (h c) -> p h c", c=65)
                    v.tensor_copy(mz3[:, :, 0:64],
                                  pb[0][0:64, :].rearrange("p (h c) -> p h c",
                                                           c=64))
                    v.tensor_scalar_add(mz3[:, :, 64:65],
                                        pb[1][0:64, 0:8].unsqueeze(2),
                                        1.0 / 64.0).then_inc(s_me, 1)
                for u in range(2):
                    v.wait_ge(s_sc, 2 * t + u + 1)
                    for j in range(4):
                        v.reduce_max(negm[:, 8 * u + 2 * j:8 * u + 2 * j + 2],
                                     pb[2 + j][:, :].rearrange(
                                         "p (i c) -> p i c", c=256),
                                     axis=AX.X, negate=True)
                    v.drain()
                    v.tensor_scalar_mul(negms[:, 8 * u:8 * u + 8],
                                        negm[:, 8 * u:8 * u + 8],
                                        0.125).then_inc(s_mx, 1)
                    if u == 0 and t + 1 < nch:
                        bn_xn_stats(t + 1)
                    v.wait_ge(s_e1, 2 * t + u + 1)
                    v.reciprocal(rdb[:, 8 * u:8 * u + 8],
                                 den[:, 8 * u:8 * u + 8])
                    v.drain()
                    for h in range(8):
                        jb = 2 * (h % 2) + (h // 4)
                        ib = (h // 2) % 2
                        c = 8 * u + 2 * jb + ib
                        li = v.tensor_scalar_mul(
                            pr_bf[:, 2048 * u + 256 * h:2048 * u + 256 * h + 256],
                            e1_bf[:, 2048 * u + 256 * h:2048 * u + 256 * h + 256],
                            rdb[:, c:c + 1])
                    li.then_inc(s_pr, 1)
                    v.wait_ge(s_pt, 2 * t + u + 1)
                    for j in range(4):
                        src = pb[2 + j][:, :].rearrange(
                            "p (i kt q) -> p i kt q", i=2, kt=2)
                        dst = pT_sb[:, 2048 * u:2048 * u + 2048].rearrange(
                            "p (kt h q) -> p h kt q", kt=2, h=8)[:, 2 * j:2 * j + 2]
                        li = v.tensor_copy(dst, src)
                    li.then_inc(s_pte, 1)
                for u in range(2):
                    bkA = pb[3] if u == 0 else pb[5]
                    bkB = pb[4] if u == 0 else pb[2]
                    bank_mt = pb[5] if u == 0 else pb[3]
                    bank_ad = pb[2] if u == 0 else pb[4]
                    v.wait_ge(s_am, 2 * t + u + 1)
                    a3 = bkA[:, 0:260].rearrange("p (j c) -> p j c", c=65)
                    b3 = bkB[:, 0:260].rearrange("p (j c) -> p j c", c=65)
                    rm3 = rmb[:, 0:8].rearrange("p (j i) -> p j i", i=2)
                    v.reciprocal(rm3[:, :, 0:1], a3[:, :, 64:65])
                    v.reciprocal(rm3[:, :, 1:2], b3[:, :, 64:65])
                    v.drain()
                    v.tensor_tensor(m2t[:, :],
                                    rmb[:, 0:8].unsqueeze(2).broadcast_to(
                                        [128, 8, 64]),
                                    g_tile[:, :].rearrange("p (h c) -> p h c",
                                                           c=64), ALU.mult)
                    v.drain()
                    amn4 = amn_bf[:, :].rearrange("p (j i c) -> p j i c",
                                                  i=2, c=64)
                    m24 = m2t[:, :].rearrange("p (j i c) -> p j i c",
                                              i=2, c=64)
                    v.tensor_tensor(amn4[:, :, 0:1, :],
                                    a3[:, :, 0:64].unsqueeze(2),
                                    m24[:, :, 0:1, :], ALU.mult)
                    v.tensor_tensor(amn4[:, :, 1:2, :],
                                    b3[:, :, 0:64].unsqueeze(2),
                                    m24[:, :, 1:2, :],
                                    ALU.mult).then_inc(s_amn, 1)
                    v.wait_ge(s_amt, 2 * t + u + 1)
                    v.tensor_copy(amT_sb[:, :], bank_mt[:, :])
                    v.drain()
                    v.wait_ge(s_ad, 2 * t + u + 1)
                    for ft in range(4):
                        li = v.scalar_tensor_tensor(
                            attT_bf[:, 128 * ft:128 * ft + 128],
                            bank_ad[:, 128 * ft:128 * ft + 128],
                            omg[:, ft:ft + 1],
                            amT_sb[:, 128 * ft:128 * ft + 128],
                            ALU.mult, ALU.add)
                    li.then_inc(s_bl, 1)
                    if u == 0 and t + 1 < nch:
                        bn_xn_affine(t + 1)
                    v.wait_ge(s_wo, 2 * t + u + 1)
                    if t >= 2 and u == 0:
                        v.wait_ge(s_o[t % 2], 16 * (t // 2))
                    bh0 = pb[2] if u == 0 else pb[4]
                    bh1 = pb[3] if u == 0 else pb[5]
                    v.tensor_add(out_sb[:, pp + 1024 * u:pp + 1024 * u + 512],
                                 x_sb[:, pp + 1024 * u:pp + 1024 * u + 512],
                                 bh0[:, :])
                    v.tensor_add(out_sb[:, pp + 1024 * u + 512:
                                        pp + 1024 * u + 1024],
                                 x_sb[:, pp + 1024 * u + 512:
                                      pp + 1024 * u + 1024],
                                 bh1[:, :]).then_inc(s_r1, 1)
                if u == 1 and t >= 1:
                    ffn_dve(v, t - 1, 0)
                    ffn_dve(v, t - 1, 1)
            for u in range(2):
                ffn_dve(v, nch - 1, u)

    return nc


def kernel(L0_attn_x, L0_attn_Wq, L0_attn_Wk, L0_attn_Wv, L0_attn_betas,
           L0_attn_Wo, L0_ffn_theta, L0_ffn_phi, L0_ffn_theta_noise,
           L0_ffn_phi_noise):
    x = np.asarray(L0_attn_x)
    nc = build_nc()
    common = {
        "wq": np.ascontiguousarray(L0_attn_Wq),
        "wk": np.ascontiguousarray(L0_attn_Wk),
        "wv": np.ascontiguousarray(L0_attn_Wv),
        "wo": np.ascontiguousarray(L0_attn_Wo),
        "betas": np.ascontiguousarray(np.asarray(L0_attn_betas).reshape(1, 512)),
        "th": np.ascontiguousarray(L0_ffn_theta),
        "thn": np.ascontiguousarray(L0_ffn_theta_noise),
        "ph": np.ascontiguousarray(L0_ffn_phi),
        "phn": np.ascontiguousarray(L0_ffn_phi_noise),
    }
    in_maps = [dict(common, x=np.ascontiguousarray(x[b])) for b in range(NCORE)]
    res = run_bass_kernel_spmd(nc, in_maps, core_ids=list(range(NCORE)))
    out = np.stack([np.asarray(res.results[b]["y"]) for b in range(NCORE)],
                   axis=0)
    return out


# version 18 (final)
# speedup vs baseline: 1.0478x; 1.0022x over previous
import contextlib
import numpy as np
import ml_dtypes
import concourse.bass as bass
import concourse.mybir as mybir
from concourse.bass_utils import run_bass_kernel_spmd

F32 = mybir.dt.float32
BF16 = mybir.dt.bfloat16
AF = mybir.ActivationFunctionType
ALU = mybir.AluOpType
AX = mybir.AxisListType

NCORE = 8
S, D = 4096, 1024
NKT = D // 128
PI_2 = float(np.pi / 2)
NTH = 4


def build_nc(nch=16):
    nc = bass.Bass("TRN2", target_bir_lowering=False)
    x_d = nc.dram_tensor("x", [S, D], F32, kind="ExternalInput").ap()
    wq_d = nc.dram_tensor("wq", [D, 512], F32, kind="ExternalInput").ap()
    wk_d = nc.dram_tensor("wk", [D, 512], F32, kind="ExternalInput").ap()
    wv_d = nc.dram_tensor("wv", [D, 512], F32, kind="ExternalInput").ap()
    wo_d = nc.dram_tensor("wo", [512, D], F32, kind="ExternalInput").ap()
    bet_d = nc.dram_tensor("betas", [1, 512], F32, kind="ExternalInput").ap()
    th_d = nc.dram_tensor("th", [D, D], F32, kind="ExternalInput").ap()
    thn_d = nc.dram_tensor("thn", [D, D], F32, kind="ExternalInput").ap()
    ph_d = nc.dram_tensor("ph", [D, D], F32, kind="ExternalInput").ap()
    phn_d = nc.dram_tensor("phn", [D, D], F32, kind="ExternalInput").ap()
    y_d = nc.dram_tensor("y", [S, D], F32, kind="ExternalOutput").ap()
    id_d = nc.inline_tensor(np.eye(128, dtype=ml_dtypes.bfloat16), "identd").ap()

    es = contextlib.ExitStack()
    sbt = lambda name, shape, dt: es.enter_context(nc.sbuf_tensor(name, shape, dt))
    pst = lambda name, shape: es.enter_context(nc.psum_tensor(name, shape, F32))
    sem = lambda name: es.enter_context(nc.semaphore(name))

    wq_b = sbt("wq_b", [128, 4096], BF16)
    wk_b = sbt("wk_b", [128, 4096], BF16)
    wv_r = sbt("wv_r", [128, 4096], BF16)
    wo_r = sbt("wo_r", [128, 4096], BF16)
    cos_r = sbt("cos_r", [128, 8192], BF16)
    sin_r = sbt("sin_r", [128, 8192], BF16)
    stage = sbt("stage", [128, 2048], F32)
    stageB = sbt("stageB", [128, 2048], F32)
    x_sb = sbt("x_sb", [128, 4096], F32)
    out_sb = sbt("out_sb", [128, 4096], F32)
    xn_bf = sbt("xn_bf", [128, 2048], BF16)
    xnT_sb = sbt("xnT_sb", [128, 4096], BF16)
    qT_bf = sbt("qT_bf", [128, 1024], BF16)
    sqT_bf = sbt("sqT_bf", [128, 1024], BF16)
    kT_bf = sbt("kT_bf", [128, 1024], BF16)
    sk_bf = sbt("sk_bf", [128, 1024], BF16)
    v_bf = sbt("v_bf", [128, 1024], BF16)
    e1_bf = sbt("e1_bf", [128, 4096], BF16)
    pr_bf = sbt("pr_bf", [128, 4096], BF16)
    pT_sb = sbt("pT_sb", [128, 4096], BF16)
    amn_bf = sbt("amn_bf", [128, 512], BF16)
    amT_sb = sbt("amT_sb", [128, 512], BF16)
    attT_bf = sbt("attT_bf", [128, 512], BF16)
    scr = sbt("scr", [128, 4096], F32)
    memz_bf = sbt("memz_bf", [128, 520], BF16)
    g_tile = sbt("g_tile", [128, 512], BF16)
    gcol = sbt("gcol", [128, 4], F32)
    omg = sbt("omg", [128, 4], F32)
    brow = sbt("brow", [1, 512], F32)
    grow_bf = sbt("grow_bf", [1, 512], BF16)
    onesr_bf = sbt("onesr_bf", [1, 128], BF16)
    onesk_bf = sbt("onesk_bf", [128, 1], BF16)
    ident_sb = sbt("ident_sb", [128, 128], BF16)
    stat6 = sbt("stat6", [128, 24], F32)
    smv = sbt("smv", [128, 4], F32)
    lnv = sbt("lnv", [128, 2], F32)
    rstd = sbt("rstd", [128, 2], F32)
    negr = sbt("negr", [128, 2], F32)
    negmr = sbt("negmr", [128, 2], F32)
    negm = sbt("negm", [128, 16], F32)
    negms = sbt("negms", [128, 16], F32)
    den = sbt("den", [128, 16], F32)
    rdb = sbt("rdb", [128, 16], F32)
    rmb = sbt("rmb", [128, 8], F32)
    m2t = sbt("m2t", [128, 512], F32)
    cbias = sbt("cbias", [128, 3], F32)
    fscr = sbt("fscr", [128, 2048], BF16)
    magb = sbt("magb", [128, 2048], BF16)

    pb = [pst(f"pb{i}", [128, 512]) for i in range(8)]

    s_x = [sem("s_x0"), sem("s_x1")]; s_o = [sem("s_o0"), sem("s_o1")]; s_mcp = sem("s_mcp")
    s_wg = sem("s_wg"); s_bld = sem("s_bld")
    s_tb = [sem("s_tb0"), sem("s_tb1")]; s_ta = [sem("s_ta0"), sem("s_ta1")]
    s_ia = sem("s_ia"); s_iv = sem("s_iv"); s_ip = sem("s_ip"); s_ig = sem("s_ig")
    s_ln = sem("s_ln"); s_xa = sem("s_xa"); s_lnc = sem("s_lnc")
    s_xnt = sem("s_xnt"); s_xne = sem("s_xne")
    s_qt = sem("s_qt"); s_qte = sem("s_qte")
    s_mq = sem("s_mq"); s_eq = sem("s_eq"); s_sq = sem("s_sq")
    s_kt = sem("s_kt"); s_kte = sem("s_kte"); s_knt = sem("s_knt")
    s_mk = sem("s_mk"); s_ek = sem("s_ek"); s_sk = sem("s_sk")
    s_v = sem("s_v"); s_ve = sem("s_ve")
    s_sc = sem("s_sc"); s_mx = sem("s_mx"); s_e1 = sem("s_e1")
    s_pr = sem("s_pr"); s_pt = sem("s_pt"); s_pte = sem("s_pte")
    s_me = sem("s_me"); s_am = sem("s_am"); s_amn = sem("s_amn")
    s_amt = sem("s_amt"); s_ad = sem("s_ad"); s_bl = sem("s_bl")
    s_wo = sem("s_wo"); s_r1 = sem("s_r1")
    s_ff = sem("s_ff"); s_sqf = sem("s_sqf"); s_ss = sem("s_ss")
    s_mag = sem("s_mag"); s_r2 = sem("s_r2"); s_md = sem("s_md")

    blk = es.enter_context(nc.Block())
    with es:
        def ffn_mms(t_, tf, u):
            ppf = 2048 * (tf % 2)
            for mi, w_s in enumerate((cos_r, sin_r)):
                if tf == 0 and u == 0:
                    t_.wait_ge(s_ia, 6 if mi == 0 else 10)
                for half in range(2):
                    n = 8 * tf + 4 * u + 2 * mi + half
                    if n >= 2:
                        t_.wait_ge(s_sqf, n - 1)
                    for dt in range(NKT):
                        li = t_.matmul(pb[6 + half][:, :],
                                  xnT_sb[:, ppf + 256 * dt + 128 * u:
                                         ppf + 256 * dt + 128 * u + 128],
                                  w_s[:, 1024 * dt + 512 * half:
                                      1024 * dt + 512 * half + 512],
                                  start=(dt == 0), stop=(dt == NKT - 1))
                    li.then_inc(s_ff, 1)

        def ffn_evac(sc, tf, u):
            if tf >= 1 or u >= 1:
                sc.wait_ge(s_ss, 2 * tf + u)
            sc.drain()
            for mi in range(2):
                for half in range(2):
                    sc.wait_ge(s_ff, 8 * tf + 4 * u + 2 * mi + half + 1)
                    sc.activation(fscr[:, 1024 * mi + 512 * half:
                                       1024 * mi + 512 * half + 512],
                                  pb[6 + half][:, :],
                                  AF.Copy).then_inc(s_sqf, 1)

        def ffn_mag(sc, tf, u):
            if tf >= 1:
                sc.wait_ge(s_r2, 2 * (tf - 1) + u + 1)
            sc.wait_ge(s_ss, 2 * tf + u + 1)
            sc.drain()
            sc.activation(stage[:, 1024:2048], stage[:, 0:1024], AF.Ln,
                          bias=cbias[:, 2:3])
            sc.drain()
            sc.activation(magb[:, 1024 * u:1024 * u + 1024],
                          stage[:, 1024:2048], AF.Exp,
                          scale=0.5).then_inc(s_mag, 1)

        def ffn_dve(v, tf, u):
            ppf = 2048 * (tf % 2)
            v.wait_ge(s_mag, 2 * tf + u + 1)
            v.tensor_add(out_sb[:, ppf + 1024 * u:ppf + 1024 * u + 1024],
                         out_sb[:, ppf + 1024 * u:ppf + 1024 * u + 1024],
                         magb[:, 1024 * u:1024 * u + 1024]).then_inc(s_r2, 1)

        @blk.sync
        def _(s):
            s.dma_start(ident_sb[:, :], id_d[:, :]).then_inc(s_bld, 16)
            s.dma_start(brow[:, :], bet_d[:, :]).then_inc(s_bld, 16)
            for t in range(min(2, nch)):
                src = x_d[256 * t:256 * (t + 1), :].rearrange(
                    "(u p) d -> p u d", u=2)
                dstx = x_sb[:, 2048 * (t % 2):2048 * (t % 2) + 2048].rearrange(
                    "p (u d) -> p u d", u=2)
                s.dma_start(dstx, src).then_inc(s_x[t % 2], 16)
            for t in range(nch):
                if t >= 1:
                    s.wait_ge(s_me, t)
                    s.dma_start(memz_bf[64:128, :],
                                memz_bf[0:64, :]).then_inc(s_mcp, 16)
                if t + 2 < nch:
                    s.wait_ge(s_r1, 2 * t + 2)
                    tt = t + 2
                    src = x_d[256 * tt:256 * (tt + 1), :].rearrange(
                        "(u p) d -> p u d", u=2)
                    dstx = x_sb[:, 2048 * (tt % 2):
                                2048 * (tt % 2) + 2048].rearrange(
                        "p (u d) -> p u d", u=2)
                    s.dma_start(dstx, src).then_inc(s_x[t % 2], 16)
                if t >= 1:
                    s.wait_ge(s_r2, 2 * t)
                    tt = t - 1
                    dst = y_d[256 * tt:256 * (tt + 1), :].rearrange(
                        "(u p) d -> p u d", u=2)
                    srco = out_sb[:, 2048 * (tt % 2):
                                  2048 * (tt % 2) + 2048].rearrange(
                        "p (u d) -> p u d", u=2)
                    s.dma_start(dst, srco).then_inc(s_o[tt % 2], 16)
            s.wait_ge(s_r2, 2 * nch)
            tt = nch - 1
            dst = y_d[256 * tt:256 * (tt + 1), :].rearrange(
                "(u p) d -> p u d", u=2)
            srco = out_sb[:, 2048 * (tt % 2):2048 * (tt % 2) + 2048].rearrange(
                "p (u d) -> p u d", u=2)
            s.dma_start(dst, srco).then_inc(s_o[tt % 2], 16)
            s.wait_ge(s_o[0], 16 * ((nch + 1) // 2))
            s.wait_ge(s_o[1], 16 * (nch // 2))

        @blk.gpsimd
        def _(g):
            g.memset(memz_bf[:, :], 0.0)
            g.drain()
            g.memset(memz_bf[:, :].rearrange("p (h c) -> p h c", c=65)
                     [:, :, 64:65], 1.0 / 64.0)
            g.memset(onesr_bf[:, :], 1.0)
            g.memset(cbias[:, 0:1], PI_2)
            g.memset(cbias[:, 1:2], 1e-5)
            g.memset(cbias[:, 2:3], 1e-38)
            g.memset(onesk_bf[:, :], 1.0)
            g.drain().then_inc(s_ig, 1)
            for w_d, w_s in ((wq_d, wq_b), (wk_d, wk_b), (wv_d, wv_r),
                             (wo_d, wo_r)):
                g.dma_start(w_s[:, :].rearrange("p (k m) -> p k m",
                                                 k=w_d.shape[0] // 128),
                            w_d[:, :].rearrange("(k p) m -> p k m",
                                                p=128)).then_inc(s_wg, 16)
            srcs = []
            for a_d, n_d in ((th_d, thn_d), (ph_d, phn_d)):
                for p in range(NTH):
                    srcs.append((
                        a_d[256 * p:256 * (p + 1), :].rearrange(
                            "(gg pp) d -> pp gg d", pp=128),
                        n_d[256 * p:256 * (p + 1), :].rearrange(
                            "(gg pp) d -> pp gg d", pp=128)))
            stg3 = [stage[:, :].rearrange("p (gg d) -> p gg d", gg=2),
                    stageB[:, :].rearrange("p (gg d) -> p gg d", gg=2)]

            def emit_accum(m):
                g.wait_ge(s_tb[m % 2], 16 * (m // 2 + 1))
                g.dma_start(stg3[m % 2], srcs[m][1],
                            accum_op=ALU.add).then_inc(s_ta[m % 2], 16)

            for n in range(2 * NTH):
                if n >= 2:
                    g.wait_ge(s_ia, n + 1)
                g.dma_start(stg3[n % 2], srcs[n][0]).then_inc(s_tb[n % 2], 16)
                if n >= 1:
                    emit_accum(n - 1)
            emit_accum(2 * NTH - 1)
            for t in range(nch):
                for u in range(2):
                    if t == 0 and u == 0:
                        g.wait_ge(s_ia, 10)
                    if 2 * t + u >= 1:
                        g.wait_ge(s_mag, 2 * t + u)
                    g.wait_ge(s_sqf, 8 * t + 4 * u + 2)
                    g.tensor_mul(stage[:, 0:1024], fscr[:, 0:1024],
                                 fscr[:, 0:1024])
                    g.wait_ge(s_sqf, 8 * t + 4 * u + 4)
                    g.drain()
                    g.tensor_mul(stage[:, 1024:2048], fscr[:, 1024:2048],
                                 fscr[:, 1024:2048])
                    g.drain()
                    g.tensor_add(stage[:, 0:1024], stage[:, 0:1024],
                                 stage[:, 1024:2048]).then_inc(s_ss, 1)

        @blk.scalar
        def _(sc):
            sc.wait_ge(s_bld, 32)
            sc.activation(grow_bf[:, :], brow[:, :], AF.Sigmoid).then_inc(s_ia, 1)
            sc.drain().then_inc(s_ia, 1)

            def emit_sin(n):
                i, p = divmod(n, NTH)
                dst = cos_r if i == 0 else sin_r
                buf = stage if n % 2 == 0 else stageB
                sc.wait_ge(s_ta[n % 2], 16 * (n // 2 + 1))
                if n == 0:
                    sc.wait_ge(s_ig, 1)
                if i == 0:
                    sc.activation(dst[:, 2048 * p:2048 * (p + 1)],
                                  buf[:, :], AF.Sin,
                                  bias=cbias[:, 0:1]).then_inc(s_ia, 1)
                else:
                    sc.activation(dst[:, 2048 * p:2048 * (p + 1)],
                                  buf[:, :], AF.Sin).then_inc(s_ia, 1)

            smv3 = smv[:, :].rearrange("p (u c) -> p u c", c=2)

            def ln_chain(tt):
                sc.wait_ge(s_ln, tt + 1)
                if tt >= 1:
                    sc.wait_ge(s_xa, 2 * tt)
                sc.activation(lnv[:, :], smv3[:, :, 1:2], AF.Ln, bias=cbias[:, 1:2])
                sc.drain()
                sc.activation(rstd[:, :], lnv[:, :], AF.Exp, scale=-0.5)
                sc.drain()
                sc.activation(negr[:, :], rstd[:, :], AF.Copy, scale=-1.0)
                sc.drain()
                for u in range(2):
                    sc.activation(negmr[:, u:u + 1], smv[:, 2 * u:2 * u + 1],
                                  AF.Copy, scale=negr[:, u:u + 1])
                sc.drain().then_inc(s_lnc, 1)

            ln_chain(0)
            for n in range(2 * NTH):
                emit_sin(n)
            for t in range(nch):
                pp = 2048 * (t % 2)
                sc.wait_ge(s_mq, t + 1)
                sc.activation(scr[:, 1024:2048], scr[:, 0:1024],
                              AF.Exp).then_inc(s_eq, 1)
                sc.wait_ge(s_mk, t + 1)
                sc.activation(scr[:, 3072:4096], scr[:, 2048:3072],
                              AF.Exp).then_inc(s_ek, 1)
                for u in range(2):
                    sc.wait_ge(s_mx, 2 * t + u + 1)
                    for h in range(8):
                        jb = 2 * (h % 2) + (h // 4)
                        ib = (h // 2) % 2
                        c = 8 * u + 2 * jb + ib
                        sl = pb[2 + jb][:, 256 * ib:256 * ib + 256]
                        sc.activation(e1_bf[:, 2048 * u + 256 * h:
                                            2048 * u + 256 * h + 256],
                                      sl, AF.Exp, bias=negms[:, c:c + 1],
                                      scale=0.125, accum_out=den[:, c:c + 1])
                    sc.drain().then_inc(s_e1, 1)
                    if t >= 1:
                        ffn_evac(sc, t - 1, u)
                        if u == 1:
                            ffn_mag(sc, t - 1, 0)
                            ffn_mag(sc, t - 1, 1)
                    if u == 0 and t + 1 < nch:
                        ln_chain(t + 1)
            for u in range(2):
                ffn_evac(sc, nch - 1, u)
            ffn_mag(sc, nch - 1, 0)
            ffn_mag(sc, nch - 1, 1)

        @blk.tensor
        def _(t_):
            t_.wait_ge(s_ia, 2)
            t_.wait_ge(s_ig, 1)
            t_.wait_ge(s_bld, 32)
            t_.matmul(pb[2][:, :], onesr_bf[0:1, :], grow_bf[0:1, :],
                      start=True, stop=True).then_inc(s_ip, 1)
            for ft in range(4):
                t_.matmul(pb[3][:, ft:ft + 1],
                          grow_bf[0:1, 128 * ft:128 * (ft + 1)],
                          onesr_bf[0:1, 0:1], start=True,
                          stop=True).then_inc(s_ip, 1)
            t_.wait_ge(s_iv, 2)
            for t in range(nch):
                pp = 2048 * (t % 2)
                t_.wait_ge(s_xa, 2 * t + 2)
                if t >= 1:
                    t_.wait_ge(s_amn, 2 * t)
                    t_.wait_ge(s_bl, 2 * t)
                for dt in range(NKT):
                    if dt == 4 and t >= 1:
                        t_.wait_ge(s_r1, 2 * t)
                    for u in range(2):
                        li = t_.matmul(pb[2 + dt // 2][:, 128 * (2 * (dt % 2) + u):
                                                  128 * (2 * (dt % 2) + u) + 128],
                                  xn_bf[:, 1024 * u + 128 * dt:
                                        1024 * u + 128 * dt + 128],
                                  ident_sb[:, :], start=True, stop=True)
                li.then_inc(s_xnt, 1)
                t_.wait_ge(s_xne, t + 1)
                if t == 0:
                    t_.wait_ge(s_wg, 64)
                for mt in range(4):
                    for kt in range(NKT):
                        li = t_.matmul(pb[2 + mt // 2][:, 256 * (mt % 2):
                                                  256 * (mt % 2) + 256],
                                  wq_b[:, 512 * kt + 128 * mt:
                                       512 * kt + 128 * mt + 128],
                                  xnT_sb[:, pp + 256 * kt:pp + 256 * kt + 256],
                                  start=(kt == 0), stop=(kt == NKT - 1))
                li.then_inc(s_qt, 1)
                if t == 0:
                    t_.wait_ge(s_wg, 64)
                for mt in range(4):
                    for kt in range(NKT):
                        li = t_.matmul(pb[4 + mt // 2][:, 256 * (mt % 2):
                                                  256 * (mt % 2) + 256],
                                  wk_b[:, 512 * kt + 128 * mt:
                                       512 * kt + 128 * mt + 128],
                                  xnT_sb[:, pp + 256 * kt:pp + 256 * kt + 256],
                                  start=(kt == 0), stop=(kt == NKT - 1))
                li.then_inc(s_kt, 1)
                t_.wait_ge(s_qte, t + 1)
                if t == 0:
                    t_.wait_ge(s_wg, 64)
                for u in range(2):
                    for kt in range(NKT):
                        li = t_.matmul(pb[2 + u][:, :],
                                  xnT_sb[:, pp + 256 * kt + 128 * u:
                                         pp + 256 * kt + 128 * u + 128],
                                  wv_r[:, 512 * kt:512 * kt + 512],
                                  start=(kt == 0), stop=(kt == NKT - 1))
                    li.then_inc(s_v, 1)
                t_.wait_ge(s_kte, t + 1)
                if t >= 2:
                    t_.wait_ge(s_sqf, 8 * (t - 1))
                for u in range(2):
                    for ft in range(4):
                        li = t_.matmul(pb[6 + u][:, 128 * ft:128 * ft + 128],
                                  kT_bf[:, 256 * ft + 128 * u:
                                        256 * ft + 128 * u + 128],
                                  ident_sb[:, :], start=True, stop=True)
                li.then_inc(s_knt, 1)
                for u in range(2):
                    if u == 0:
                        t_.wait_ge(s_ve, 2 * t + 1)
                    else:
                        t_.wait_ge(s_pte, 2 * t + 1)
                    for h in (0, 2, 4, 6, 1, 3, 5, 7):
                        if u == 0 and h == 4:
                            t_.wait_ge(s_ve, 2 * t + 2)
                        b = 64 * (h % 2)
                        jb = 2 * (h % 2) + (h // 4)
                        ib = (h // 2) % 2
                        li = t_.matmul(pb[2 + jb][:, 256 * ib:256 * ib + 256],
                                  qT_bf[b:b + 64, 256 * (h // 2) + 128 * u:
                                        256 * (h // 2) + 128 * u + 128],
                                  kT_bf[b:b + 64, 256 * (h // 2):
                                        256 * (h // 2) + 256],
                                  start=True, stop=True)
                    li.then_inc(s_sc, 1)
                    if t >= 1:
                        if u == 0:
                            t_.wait_ge(s_sk, t + 1)
                        ffn_mms(t_, t - 1, u)
                    t_.wait_ge(s_pr, 2 * t + u + 1)
                    for h in range(8):
                        for kt in range(2):
                            li = t_.matmul(pb[2 + h // 2][:, 256 * (h % 2) + 128 * kt:
                                                     256 * (h % 2) + 128 * kt + 128],
                                      pr_bf[:, 2048 * u + 256 * h + 128 * kt:
                                            2048 * u + 256 * h + 128 * kt + 128],
                                      ident_sb[:, :], start=True, stop=True)
                    li.then_inc(s_pt, 1)
                for u in range(2):
                    bank_ad = pb[2] if u == 0 else pb[4]
                    t_.wait_ge(s_pte, 2 * t + 2)
                    if u == 1:
                        t_.wait_ge(s_bl, 2 * t + 1)
                        t_.wait_ge(s_amn, 2 * t + 1)
                        t_.wait_ge(s_r1, 2 * t + 1)
                    for kt in range(2):
                        for h in range(8):
                            b = 64 * (h % 2)
                            li = t_.matmul(bank_ad[b:b + 64, 128 * (h // 2):
                                              128 * (h // 2) + 128],
                                      v_bf[:, 512 * kt + 64 * h:
                                           512 * kt + 64 * h + 64],
                                      pT_sb[:, 2048 * u + 1024 * kt + 128 * h:
                                            2048 * u + 1024 * kt + 128 * h + 128],
                                      start=(kt == 0 and h <= 1),
                                      stop=(kt == 1 and h >= 6),
                                      skip_group_check=True)
                    li.then_inc(s_ad, 1)
                    if t >= 1:
                        t_.wait_ge(s_me, t)
                        t_.wait_ge(s_mcp, 16 * t)
                    else:
                        t_.wait_ge(s_ig, 1)
                    t_.wait_ge(s_sq, t + 1)
                    bkA = pb[3] if u == 0 else pb[5]
                    bkB = pb[4] if u == 0 else pb[2]
                    for h in range(8):
                        bank = bkA if h % 2 == 0 else bkB
                        col = 65 * (h // 2)
                        b = 64 * (h % 2)
                        li = t_.matmul(bank[:, col:col + 65],
                                  sqT_bf[b:b + 64, 256 * (h // 2) + 128 * u:
                                         256 * (h // 2) + 128 * u + 128],
                                  memz_bf[b:b + 64, 65 * h:65 * h + 65],
                                  start=True, stop=True)
                    li.then_inc(s_am, 1)
                    bank_mt = pb[5] if u == 0 else pb[3]
                    t_.wait_ge(s_amn, 2 * t + u + 1)
                    for ft in range(4):
                        li = t_.matmul(bank_mt[:, 128 * ft:128 * ft + 128],
                                  amn_bf[:, 128 * ft:128 * ft + 128],
                                  ident_sb[:, :], start=True, stop=True)
                    li.then_inc(s_amt, 1)
                    t_.wait_ge(s_bl, 2 * t + u + 1)
                    if t == 0 and u == 0:
                        t_.wait_ge(s_wg, 64)
                    bh0 = pb[2] if u == 0 else pb[4]
                    bh1 = pb[3] if u == 0 else pb[5]
                    for half in range(2):
                        for ft in range(4):
                            li = t_.matmul((bh0 if half == 0 else bh1)[:, :],
                                      attT_bf[:, 128 * ft:128 * ft + 128],
                                      wo_r[:, 1024 * ft + 512 * half:
                                           1024 * ft + 512 * half + 512],
                                      start=(ft == 0), stop=(ft == 3))
                    li.then_inc(s_wo, 1)
                if t >= 1:
                    t_.wait_ge(s_me, t)
                for kt in range(2):
                    for h in range(8):
                        t_.matmul(pb[0][0:64, 64 * h:64 * h + 64],
                                  sk_bf[:, 512 * kt + 64 * h:
                                        512 * kt + 64 * h + 64],
                                  v_bf[:, 512 * kt + 64 * h:
                                       512 * kt + 64 * h + 64],
                                  start=(t == 0 and kt == 0 and h == 0),
                                  stop=(t == nch - 1 and kt == 1 and h == 7),
                                  skip_group_check=True)
                        li = t_.matmul(pb[1][0:64, h:h + 1],
                                  sk_bf[:, 512 * kt + 64 * h:
                                        512 * kt + 64 * h + 64],
                                  onesk_bf[:, :],
                                  start=(t == 0 and kt == 0 and h == 0),
                                  stop=(t == nch - 1 and kt == 1 and h == 7),
                                  skip_group_check=True)
                li.then_inc(s_md, 1)
            t_.wait_ge(s_sk, nch)
            for u in range(2):
                ffn_mms(t_, nch - 1, u)

        @blk.vector
        def _(v):
            v.wait_ge(s_ip, 1)
            v.tensor_copy(g_tile[:, :], pb[2][:, :]).then_inc(s_iv, 1)
            v.wait_ge(s_ip, 5)
            v.tensor_copy(gcol[:, :], pb[3][:, 0:4])
            v.drain()
            v.tensor_scalar(omg[:, :], gcol[:, :], -1.0, 1.0, ALU.mult, ALU.add)
            v.drain().then_inc(s_iv, 1)
            def bn_xn_stats(tt):
                pq = 2048 * (tt % 2)
                v.wait_ge(s_x[tt % 2], 16 * (tt // 2 + 1))
                if tt >= 1:
                    v.wait_ge(s_lnc, tt)
                for u in range(2):
                    v.bn_stats(stat6[:, 12 * u:12 * u + 6],
                               x_sb[:, pq + 1024 * u:pq + 1024 * u + 512])
                    v.bn_stats(stat6[:, 12 * u + 6:12 * u + 12],
                               x_sb[:, pq + 1024 * u + 512:
                                    pq + 1024 * u + 1024])
                    v.drain()
                    li = v.bn_aggr(smv[:, 2 * u:2 * u + 2],
                                   stat6[:, 12 * u:12 * u + 12])
                li.then_inc(s_ln, 1)

            def bn_xn_affine(tt):
                pq = 2048 * (tt % 2)
                v.wait_ge(s_lnc, tt + 1)
                if tt >= 1:
                    v.wait_ge(s_xnt, tt)
                for u in range(2):
                    v.tensor_scalar(xn_bf[:, 1024 * u:1024 * (u + 1)],
                                    x_sb[:, pq + 1024 * u:pq + 1024 * (u + 1)],
                                    rstd[:, u:u + 1], negmr[:, u:u + 1],
                                    ALU.mult, ALU.add).then_inc(s_xa, 1)

            bn_xn_stats(0)
            bn_xn_affine(0)
            for t in range(nch):
                pp = 2048 * (t % 2)
                v.wait_ge(s_xnt, t + 1)
                if t >= 2:
                    v.wait_ge(s_ff, 8 * (t - 2) + 8)
                for j in range(4):
                    li = v.tensor_copy(xnT_sb[:, pp + 512 * j:pp + 512 * j + 512],
                                       pb[2 + j][:, :])
                li.then_inc(s_xne, 1)
                v.wait_ge(s_qt, t + 1)
                v.tensor_copy(qT_bf[:, 0:512], pb[2][:, :])
                v.tensor_copy(qT_bf[:, 512:1024],
                              pb[3][:, :]).then_inc(s_qte, 1)
                v.drain()
                v.tensor_scalar_min(scr[:, 0:1024], qT_bf[:, :],
                                    0.0).then_inc(s_mq, 1)
                v.wait_ge(s_kt, t + 1)
                v.tensor_copy(kT_bf[:, 0:512], pb[4][:, :])
                v.tensor_copy(kT_bf[:, 512:1024],
                              pb[5][:, :]).then_inc(s_kte, 1)
                v.wait_ge(s_eq, t + 1)
                v.scalar_tensor_tensor(sqT_bf[:, :], qT_bf[:, :], 0.0,
                                       scr[:, 1024:2048], ALU.max,
                                       ALU.add).then_inc(s_sq, 1)
                for u in range(2):
                    v.wait_ge(s_v, 2 * t + u + 1)
                    v.tensor_copy(v_bf[:, 512 * u:512 * u + 512],
                                  pb[2 + u][:, :]).then_inc(s_ve, 1)
                v.wait_ge(s_knt, t + 1)
                for u in range(2):
                    li = v.tensor_scalar_min(scr[:, 2048 + 512 * u:
                                                 2048 + 512 * u + 512],
                                             pb[6 + u][:, :], 0.0)
                li.then_inc(s_mk, 1)
                v.wait_ge(s_ek, t + 1)
                for u in range(2):
                    li = v.scalar_tensor_tensor(sk_bf[:, 512 * u:512 * u + 512],
                                                pb[6 + u][:, :], 0.0,
                                                scr[:, 3072 + 512 * u:
                                                    3072 + 512 * u + 512],
                                                ALU.max, ALU.add)
                li.then_inc(s_sk, 1)
                if t >= 1:
                    v.wait_ge(s_md, t)
                    mz3 = memz_bf[0:64, :].rearrange("p (h c) -> p h c", c=65)
                    v.tensor_copy(mz3[:, :, 0:64],
                                  pb[0][0:64, :].rearrange("p (h c) -> p h c",
                                                           c=64))
                    v.tensor_scalar_add(mz3[:, :, 64:65],
                                        pb[1][0:64, 0:8].unsqueeze(2),
                                        1.0 / 64.0).then_inc(s_me, 1)
                for u in range(2):
                    v.wait_ge(s_sc, 2 * t + u + 1)
                    for j in range(4):
                        v.reduce_max(negm[:, 8 * u + 2 * j:8 * u + 2 * j + 2],
                                     pb[2 + j][:, :].rearrange(
                                         "p (i c) -> p i c", c=256),
                                     axis=AX.X, negate=True)
                    v.drain()
                    v.tensor_scalar_mul(negms[:, 8 * u:8 * u + 8],
                                        negm[:, 8 * u:8 * u + 8],
                                        0.125).then_inc(s_mx, 1)
                    if u == 0 and t + 1 < nch:
                        bn_xn_stats(t + 1)
                    v.wait_ge(s_e1, 2 * t + u + 1)
                    v.reciprocal(rdb[:, 8 * u:8 * u + 8],
                                 den[:, 8 * u:8 * u + 8])
                    v.drain()
                    for h in range(8):
                        jb = 2 * (h % 2) + (h // 4)
                        ib = (h // 2) % 2
                        c = 8 * u + 2 * jb + ib
                        li = v.tensor_scalar_mul(
                            pr_bf[:, 2048 * u + 256 * h:2048 * u + 256 * h + 256],
                            e1_bf[:, 2048 * u + 256 * h:2048 * u + 256 * h + 256],
                            rdb[:, c:c + 1])
                    li.then_inc(s_pr, 1)
                    v.wait_ge(s_pt, 2 * t + u + 1)
                    for j in range(4):
                        src = pb[2 + j][:, :].rearrange(
                            "p (i kt q) -> p i kt q", i=2, kt=2)
                        dst = pT_sb[:, 2048 * u:2048 * u + 2048].rearrange(
                            "p (kt h q) -> p h kt q", kt=2, h=8)[:, 2 * j:2 * j + 2]
                        li = v.tensor_copy(dst, src)
                    li.then_inc(s_pte, 1)
                for u in range(2):
                    bkA = pb[3] if u == 0 else pb[5]
                    bkB = pb[4] if u == 0 else pb[2]
                    bank_mt = pb[5] if u == 0 else pb[3]
                    bank_ad = pb[2] if u == 0 else pb[4]
                    v.wait_ge(s_am, 2 * t + u + 1)
                    a3 = bkA[:, 0:260].rearrange("p (j c) -> p j c", c=65)
                    b3 = bkB[:, 0:260].rearrange("p (j c) -> p j c", c=65)
                    rm3 = rmb[:, 0:8].rearrange("p (j i) -> p j i", i=2)
                    v.reciprocal(rm3[:, :, 0:1], a3[:, :, 64:65])
                    v.reciprocal(rm3[:, :, 1:2], b3[:, :, 64:65])
                    v.drain()
                    v.tensor_tensor(m2t[:, :],
                                    rmb[:, 0:8].unsqueeze(2).broadcast_to(
                                        [128, 8, 64]),
                                    g_tile[:, :].rearrange("p (h c) -> p h c",
                                                           c=64), ALU.mult)
                    v.drain()
                    amn4 = amn_bf[:, :].rearrange("p (j i c) -> p j i c",
                                                  i=2, c=64)
                    m24 = m2t[:, :].rearrange("p (j i c) -> p j i c",
                                              i=2, c=64)
                    v.tensor_tensor(amn4[:, :, 0:1, :],
                                    a3[:, :, 0:64].unsqueeze(2),
                                    m24[:, :, 0:1, :], ALU.mult)
                    v.tensor_tensor(amn4[:, :, 1:2, :],
                                    b3[:, :, 0:64].unsqueeze(2),
                                    m24[:, :, 1:2, :],
                                    ALU.mult).then_inc(s_amn, 1)
                    v.wait_ge(s_amt, 2 * t + u + 1)
                    v.tensor_copy(amT_sb[:, :], bank_mt[:, :])
                    v.drain()
                    v.wait_ge(s_ad, 2 * t + u + 1)
                    for ft in range(4):
                        li = v.scalar_tensor_tensor(
                            attT_bf[:, 128 * ft:128 * ft + 128],
                            bank_ad[:, 128 * ft:128 * ft + 128],
                            omg[:, ft:ft + 1],
                            amT_sb[:, 128 * ft:128 * ft + 128],
                            ALU.mult, ALU.add)
                    li.then_inc(s_bl, 1)
                    if u == 0 and t + 1 < nch:
                        bn_xn_affine(t + 1)
                    v.wait_ge(s_wo, 2 * t + u + 1)
                    if t >= 2 and u == 0:
                        v.wait_ge(s_o[t % 2], 16 * (t // 2))
                    bh0 = pb[2] if u == 0 else pb[4]
                    bh1 = pb[3] if u == 0 else pb[5]
                    v.tensor_add(out_sb[:, pp + 1024 * u:pp + 1024 * u + 512],
                                 x_sb[:, pp + 1024 * u:pp + 1024 * u + 512],
                                 bh0[:, :])
                    v.tensor_add(out_sb[:, pp + 1024 * u + 512:
                                        pp + 1024 * u + 1024],
                                 x_sb[:, pp + 1024 * u + 512:
                                      pp + 1024 * u + 1024],
                                 bh1[:, :]).then_inc(s_r1, 1)
                if u == 1 and t >= 1:
                    ffn_dve(v, t - 1, 0)
                    ffn_dve(v, t - 1, 1)
            for u in range(2):
                ffn_dve(v, nch - 1, u)

    return nc


def kernel(L0_attn_x, L0_attn_Wq, L0_attn_Wk, L0_attn_Wv, L0_attn_betas,
           L0_attn_Wo, L0_ffn_theta, L0_ffn_phi, L0_ffn_theta_noise,
           L0_ffn_phi_noise):
    x = np.asarray(L0_attn_x)
    nc = build_nc()
    common = {
        "wq": np.ascontiguousarray(L0_attn_Wq),
        "wk": np.ascontiguousarray(L0_attn_Wk),
        "wv": np.ascontiguousarray(L0_attn_Wv),
        "wo": np.ascontiguousarray(L0_attn_Wo),
        "betas": np.ascontiguousarray(np.asarray(L0_attn_betas).reshape(1, 512)),
        "th": np.ascontiguousarray(L0_ffn_theta),
        "thn": np.ascontiguousarray(L0_ffn_theta_noise),
        "ph": np.ascontiguousarray(L0_ffn_phi),
        "phn": np.ascontiguousarray(L0_ffn_phi_noise),
    }
    in_maps = [dict(common, x=np.ascontiguousarray(x[b])) for b in range(NCORE)]
    res = run_bass_kernel_spmd(nc, in_maps, core_ids=list(range(NCORE)))
    out = np.stack([np.asarray(res.results[b]["y"]) for b in range(NCORE)],
                   axis=0)
    return out
